# version 16
# baseline (speedup 1.0000x reference)
import sys
sys.path.insert(0, "/opt/trn_rl_repo")
import contextlib
import os
import numpy as np
import ml_dtypes

import concourse.bass as bass
import concourse.mybir as mybir
from concourse.bass_utils import run_bass_kernel_spmd

F32 = mybir.dt.float32
BF16 = mybir.dt.bfloat16
AF = mybir.ActivationFunctionType
ALU = mybir.AluOpType

NCORE = 8
T = 8192
D = 1024
DI = 2048
DS = 16
DTR = 64
TC = 512
HC = 256
NCHUNK = T // TC
NDT = DI // 128
NDM = D // 128
DH = 4096
NH = DH // 128

ENG_SEMS = {"sync": "SY", "tensor": "PE", "vector": "DV", "scalar": "AC", "gpsimd": "PL"}

WSPECS = {
    "w_inT":    (8, 8, 512),
    "w_outT":   (4, 16, 256),
    "w_writeT": (2, 8, 512),
    "w_readT":  (2, 8, 512),
    "w1T":      (8, 8, 512),
    "w2T":      (8, 8, 512),
    "h1T":      (32, 32, 128),
    "h2T":      (32, 32, 128),
    "w3T":      (8, 32, 128),
}


class Buf:
    __slots__ = ("ready", "users", "groups")
    def __init__(self):
        self.ready = {}
        self.users = {}
        self.groups = []


class Group:
    def __init__(self):
        self.tag = None
        self.users = {}


DMA_SEMS = (["RS0", "RS1", "RS2", "RS3", "RF0", "RF1",
             "MSD0", "MSD1", "MSU0", "MSU1", "MSZ0", "MSZ1",
             "SPU", "SPZ", "SPDT", "SPY", "AW0", "AW1", "WTS",
             "SCW", "SYC", "X0Q", "STQ", "SYQ", "GQQ",
             "CW0", "CW1", "CW2", "CW3", "CW4"])
FULL_WAIT = {"SPU", "SPZ", "SPDT", "SPY", "WTS", "SCW", "SYC", "X0Q"}


class Planner:
    def __init__(self):
        self.ops = {e: [] for e in ENG_SEMS}
        self.count = {s: 0 for s in list(ENG_SEMS.values()) + DMA_SEMS}

    def op(self, engine, emit, reads=(), writes=(), extra=(), dma_queue=None, defer=False):
        waits = {}
        def mw(s, v):
            if v > waits.get(s, -1):
                waits[s] = v
        switched = []
        for blist, iswrite in ((reads, False), (writes, True)):
            for bb in blist:
                if iswrite:
                    for s, v in bb.users.items():
                        mw(s, v)
                for s, v in bb.ready.items():
                    mw(s, v)
                for gr, tag in bb.groups:
                    if gr.tag != tag:
                        switched.append((gr, tag))
        for gr, tag in switched:
            for s, v in gr.users.items():
                mw(s, v)
            gr.users = {}
            gr.tag = tag
        for t_ in extra:
            if t_ is not None:
                mw(t_[0], t_[1])
        for s in list(waits):
            if s in FULL_WAIT:
                waits[s] = self.count[s]
        if dma_queue is not None:
            sem = dma_queue
            self.count[sem] += 16
            tok = (sem, self.count[sem])
            inc = (sem, 16)
        elif defer:
            sem = ENG_SEMS[engine]
            tok = (sem, self.count[sem] + 1)
            inc = None
        else:
            sem = ENG_SEMS[engine]
            self.count[sem] += 1
            tok = (sem, self.count[sem])
            inc = (sem, 1)
        self.ops[engine].append((waits, emit, inc))
        for bb in reads:
            if tok[1] > bb.users.get(tok[0], -1):
                bb.users[tok[0]] = tok[1]
            for gr, _t in bb.groups:
                if tok[1] > gr.users.get(tok[0], -1):
                    gr.users[tok[0]] = tok[1]
        for bb in writes:
            bb.ready = {tok[0]: tok[1]}
            bb.users = {tok[0]: tok[1]}
            for gr, _t in bb.groups:
                if tok[1] > gr.users.get(tok[0], -1):
                    gr.users[tok[0]] = tok[1]
        return tok


NO_SCAN = False
SCAN_AS_MUL = False
GEN_LEVEL = 9
YIELD_XZ = False
YIELD_PRE = False
DRAIN_STAGE = 3
RACE_DET = True

def build_program(nchunk=NCHUNK, dbg=False):
    nc = bass.Bass("TRN2", target_bir_lowering=False, debug=False, detect_race_conditions=RACE_DET)
    es = contextlib.ExitStack()

    def din(name, shape, dt=F32):
        return nc.dram_tensor(name, shape, dt, kind="ExternalInput").ap()

    x0_in = din("x0", [nchunk, 128, NDM, TC])
    out_fm = nc.dram_tensor("out_fm", [nchunk, 128, NDM, TC], F32, kind="ExternalOutput").ap()

    w_f32, w_scr = {}, {}
    for nm, (ns, kt, ew) in WSPECS.items():
        w_f32[nm] = din(nm, [ns, 128, kt, ew])
        w_scr[nm] = nc.dram_tensor("scr_" + nm, [ns, 128, kt, ew], BF16, kind="Internal").ap()

    scr_sp = {}
    for nm in ("sdt", "su", "sz", "sy"):
        scr_sp[nm] = nc.dram_tensor("scr_" + nm, [NDT, 128, TC], BF16, kind="Internal").ap()
    scr_WT = nc.dram_tensor("scr_WT", [8, 128, 1024], F32, kind="Internal").ap()
    scr_WTb = nc.dram_tensor("scr_WTb", [8, 128, 1024], BF16, kind="Internal").ap()

    wxdt_in = din("wx_dtT", [128, NDT * DTR])
    wxb_in = din("wx_bT", [128, NDT * 16])
    wxc_in = din("wx_cT", [128, NDT * 16])
    wdtT_in = din("wdtT", [64, DI])
    convw_in = din("conv_w_l", [128, NDT * 4])
    convb_in = din("conv_b_l", [128, NDT])
    D_in = din("D_l", [128, NDT])
    bdt_in = din("bdt_l", [128, NDT])
    Alog_in = din("Alog_l", [128, NDT * DS])
    scales_in = din("scales_l", [128, 4 * NH + NDM])
    sel_in = din("sel16", [16, 16 * 128])
    diffs_in = din("diffs_l", [128, 4 * TC])
    causal_in = din("causal_l", [128, 4 * TC])
    pos_in = din("pos_row", [1, TC])
    revpos_in = din("revpos_l", [128, 4])
    ident_in = din("ident", [128, 128])
    decay_in = din("decay_s", [1, 1])
    logalpha_in = din("logalpha_s", [1, 1])

    dbg_o = {}
    if dbg == 1:
        for nm in ("dbg_x1", "dbg_x2", "dbg_x3"):
            dbg_o[nm] = nc.dram_tensor(nm, [nchunk, 128, NDM * TC], F32, kind="ExternalOutput").ap()
    if dbg == 2:
        for nm in ("dbg_sdt", "dbg_su", "dbg_sz", "dbg_sy"):
            dbg_o[nm] = nc.dram_tensor(nm, [NDT, 128, TC], BF16, kind="ExternalOutput").ap()
        dbg_o["dbg_wtb"] = nc.dram_tensor("dbg_wtb", [8, 128, 1024], BF16, kind="ExternalOutput").ap()
        dbg_o["dbg_wtf"] = nc.dram_tensor("dbg_wtf", [8, 128, 1024], F32, kind="ExternalOutput").ap()

    def salloc(name, shape, dt):
        return nc.alloc_sbuf_tensor(name, list(shape), dt)

    x_res = salloc("x_res", [128, NDM * TC], F32)
    x0_next = salloc("x0_next", [128, NDM * TC], F32)
    st_scan = salloc("st_scan", [128, NDT * DS], F32)
    st_ctail = salloc("st_ctail", [128, NDT * 3], BF16)
    st_wktail = salloc("st_wktail", [128, NDM], BF16)
    ring = [salloc(f"ring{i}", [128, 4096], BF16) for i in range(4)]
    k_M = salloc("k_M", [128, 4 * TC], BF16)
    k_gp = salloc("k_gp", [128, TC], BF16)
    k_gw = salloc("k_gw", [128, 4], F32)
    k_logg = salloc("k_logg", [128, 1], F32)
    k_gC = salloc("k_gC", [128, 1], F32)
    k_alpha = salloc("k_alpha", [128, 1], F32)
    c_convw = salloc("c_convw", [128, NDT * 4], F32)
    c_convb = salloc("c_convb", [128, NDT], F32)
    c_D = salloc("c_D", [128, NDT], F32)
    c_bdt = salloc("c_bdt", [128, NDT], F32)
    c_negA = salloc("c_negA", [128, NDT * DS], F32)
    c_scales = salloc("c_scales", [128, 4 * NH + NDM], F32)
    c_sel = salloc("c_sel", [16, 16 * 128], BF16)
    c_revpos = salloc("c_revpos", [128, 4], F32)
    c_ident = salloc("c_ident", [128, 128], BF16)
    c_ones_r = salloc("c_ones_r", [1, 128], F32)
    c_ones_c = salloc("c_ones_c", [128, 1], BF16)
    s_decay = salloc("s_decay", [1, 1], F32)
    s_logalpha = salloc("s_logalpha", [1, 1], F32)
    s_t1 = salloc("s_t1", [1, 1], F32)
    k_eps = salloc("k_eps", [128, 1], F32)
    k_half = salloc("k_half", [128, 1], F32)
    k_neghalf = salloc("k_neghalf", [128, 1], F32)
    k_invD = salloc("k_invD", [128, 1], F32)
    k_tc = salloc("k_tc", [128, 1], F32)
    l_m = salloc("l_m", [1, TC], F32)
    l_v = salloc("l_v", [1, TC], F32)
    l_r = salloc("l_r", [1, TC], F32)
    l_mr = salloc("l_mr", [1, TC], F32)
    l_rrep = salloc("l_rrep", [128, TC], F32)
    l_mrrep = salloc("l_mrrep", [128, TC], F32)
    lt_bf = [salloc("lt_bf0", [128, TC], BF16)]
    wxdt_b = salloc("wxdt_b", [128, NDT * DTR], BF16)
    wxb_b = salloc("wxb_b", [128, NDT * 16], BF16)
    wxc_b = salloc("wxc_b", [128, NDT * 16], BF16)
    wdt_b = salloc("wdt_b", [64, DI], BF16)

    ARENA_BYTES = (nc.sbuf_bytes_remaining - 256) // 32 * 32
    arena_h = nc.alloc_sbuf_tensor("arena", [128, ARENA_BYTES], mybir.dt.uint8)
    AB = nc.lookup_mloc(arena_h).addr
    assert AB is not None

    def aalloc(cur2, name, shape, dt):
        per = int(np.prod(shape[1:])) * mybir.dt.size(dt)
        off = (cur2[0] + 31) // 32 * 32
        cur2[0] = off + per
        assert cur2[0] <= ARENA_BYTES, f"arena overflow {name}: {cur2[0]} > {ARENA_BYTES}"
        return nc.alloc_sbuf_tensor_at(name, list(shape), dt, offset=AB + off)

    am = [0]
    mb_pad = aalloc(am, "mb_pad", [128, TC + 3], BF16)
    mb_ev = [aalloc(am, f"mb_ev{i}", [128, TC], BF16) for i in range(3)]
    ms_dt = [aalloc(am, f"ms_dt{i}", [128, HC], BF16) for i in range(2)]
    ms_u = [aalloc(am, f"ms_u{i}", [128, HC], BF16) for i in range(2)]
    ms_z = [aalloc(am, f"ms_z{i}", [128, HC], BF16) for i in range(2)]
    ms_y = [aalloc(am, f"ms_y{i}", [128, HC], BF16) for i in range(2)]
    m_dtraw = aalloc(am, "m_dtraw", [64, TC], BF16)
    m_b = aalloc(am, "m_b", [16, TC], BF16)
    m_c = aalloc(am, "m_c", [16, TC], BF16)
    xb_off = (am[0] + 31) // 32 * 32
    m_xln = nc.alloc_sbuf_tensor_at("m_xln", [128, NDM * TC], BF16, offset=AB + xb_off)
    m_Ball = nc.alloc_sbuf_tensor_at("m_Ball", [128, DS * HC], BF16, offset=AB + xb_off)
    am[0] = xb_off + DS * HC * 2
    m_Call = aalloc(am, "m_Call", [128, DS * HC], BF16)
    m_dtu2 = [aalloc(am, f"m_dtu2{i}", [128, HC], BF16) for i in range(2)]
    m_dA = [aalloc(am, f"m_dA{i}", [128, HC], F32) for i in range(8)]
    m_dBu = [aalloc(am, f"m_dBu{i}", [128, HC], BF16) for i in range(3)]
    m_h = [aalloc(am, f"m_h{i}", [128, HC], BF16) for i in range(3)]
    m_yacc = aalloc(am, "m_yacc", [128, HC], F32)
    m_ytmp = aalloc(am, "m_ytmp", [128, HC], BF16)
    m_t1 = aalloc(am, "m_t1", [128, TC], F32)
    m_t2 = aalloc(am, "m_t2", [128, HC], BF16)
    MS = (am[0] + 31) // 32 * 32

    ap_ = [MS]
    ring_f = [aalloc(ap_, "ringf0", [128, 2048], F32)]
    p_diffs = aalloc(ap_, "p_diffs", [128, 4 * TC], F32)
    p_causal = aalloc(ap_, "p_causal", [128, 4 * TC], F32)
    p_wf1 = aalloc(ap_, "p_wf1", [128, NDT * DTR], F32)
    p_wf2 = aalloc(ap_, "p_wf2", [128, NDT * 16], F32)
    p_wf3 = aalloc(ap_, "p_wf3", [128, NDT * 16], F32)
    p_wf4 = aalloc(ap_, "p_wf4", [64, DI], F32)
    c_pos = aalloc(ap_, "c_pos", [1, TC], F32)
    s_row = aalloc(ap_, "s_row", [1, TC], F32)

    af = [MS]
    f_h = aalloc(af, "f_h", [128, NH * TC], BF16)
    fg_off = (af[0] + 31) // 32 * 32
    f_g = nc.alloc_sbuf_tensor_at("f_g", [128, NH * TC], BF16, offset=AB + fg_off)
    f_xln = nc.alloc_sbuf_tensor_at("f_xln", [128, NDM * TC], BF16, offset=AB + fg_off)
    af[0] = fg_off + NH * TC * 2
    f_t1 = aalloc(af, "f_t1", [128, TC], F32)
    ffn_end = af[0]

    aa = [MS]
    a_rkpad = aalloc(aa, "a_rkpad", [128, NDM * (TC + 1)], BF16)
    a_vtm = aalloc(aa, "a_vtm", [128, 4 * 1024], BF16)
    a_wktm = aalloc(aa, "a_wktm", [128, 4 * 1024], BF16)
    a_SmT = aalloc(aa, "a_SmT", [128, 4 * TC], BF16)
    a_reads = aalloc(aa, "a_reads", [128, NDM * TC], BF16)
    aw_f32 = [aalloc(aa, f"aw_f32{i}", [128, 1024], F32) for i in range(2)]
    aw_bf = aalloc(aa, "aw_bf", [128, 1024], BF16)
    a_t1 = aalloc(aa, "a_t1", [128, TC], F32)
    attn_end = aa[0]

    arena_end = max(ap_[0], ffn_end, attn_end)
    if os.environ.get("ARENA_DEBUG"):
        print(f"ARENA: bytes={ARENA_BYTES} MS={MS} ap={ap_[0]} ffn={ffn_end} attn={attn_end} end={arena_end} slack={ARENA_BYTES-arena_end}")
    assert arena_end <= ARENA_BYTES, f"SBUF over: {arena_end} > {ARENA_BYTES}"

    pb = [nc.alloc_psum_tensor(f"pb{i}", [128, 512], F32) for i in range(6)]
    pb_dw = nc.alloc_psum_tensor("pb_dw", [128, 512], F32)
    pbT = nc.alloc_psum_tensor("pbT", [128, 128], BF16)

    sems = {}
    for s in ["SY", "PE", "DV", "AC", "PL"] + DMA_SEMS:
        sems[s] = es.enter_context(nc.semaphore(s))
    pl = Planner()
    G = Group()
    G2 = Group()
    G3 = Group()
    BUFS = {}

    def b(h):
        k = id(h)
        if k not in BUFS:
            BUFS[k] = Buf()
        return BUFS[k]

    def gb(h, tag, grp=None):
        bb = b(h)
        gr = G if grp is None else grp
        found = False
        for idx, (g0, _t) in enumerate(bb.groups):
            if g0 is gr:
                bb.groups[idx] = (gr, tag)
                found = True
        if not found:
            bb.groups.append((gr, tag))
        return bb

    scr_ready = Buf()
    st_bufs = [Buf() for _ in range(NDT * DS)]
    sp_bufs = {nm: [Buf() for _ in range(NDT)] for nm in ("sdt", "su", "sz", "sy")}
    wt_bufs = [Buf() for _ in range(8)]
    wtb_bufs = [Buf() for _ in range(8)]

    def sy_dma(dst, src, reads=(), writes=(), queue="SYQ"):
        return pl.op("sync", lambda e, d=dst, s=src: e.dma_start(out=d, in_=s),
                     reads=reads, writes=writes, dma_queue=queue)

    def gq_dma(dst, src, reads=(), writes=(), queue="GQQ"):
        return pl.op("gpsimd", lambda e, d=dst, s=src: e.dma_start(out=d, in_=s),
                     reads=reads, writes=writes, dma_queue=queue)

    def sc_dma(dst, src, reads=(), writes=(), queue="GQQ"):
        return pl.op("scalar", lambda e, d=dst, s=src: e.dma_start(out=d, in_=s),
                     reads=reads, writes=writes, dma_queue=queue)

    def dve(emit, reads=(), writes=()):
        return pl.op("vector", emit, reads=reads, writes=writes)

    def act(emit, reads=(), writes=()):
        return pl.op("scalar", emit, reads=reads, writes=writes)

    def pool(emit, reads=(), writes=()):
        return pl.op("gpsimd", emit, reads=reads, writes=writes)

    def pe(emit, reads=(), writes=()):
        return pl.op("tensor", emit, reads=reads, writes=writes)

    def mm_group(psum_ap, psum_buf, pairs):
        n = len(pairs)
        for i, (lt, rh, rbs) in enumerate(pairs):
            pl.op("tensor",
                  lambda e, o=psum_ap, l=lt, r=rh, st=(i == 0), sp=(i == n - 1):
                  e.matmul(o, l, r, start=st, stop=sp),
                  reads=list(rbs), writes=[psum_buf], defer=(i < n - 1))

    for dst, src, tagged in [
            (c_convw, convw_in, 0), (c_convb, convb_in, 0), (c_D, D_in, 0),
            (c_bdt, bdt_in, 0), (c_negA, Alog_in, 0), (c_scales, scales_in, 0),
            (c_pos, pos_in, 1), (c_revpos, revpos_in, 0),
            (s_decay, decay_in, 0), (s_logalpha, logalpha_in, 0),
            (p_diffs, diffs_in, 1), (p_causal, causal_in, 1),
            (p_wf1, wxdt_in, 1), (p_wf2, wxb_in, 1), (p_wf3, wxc_in, 1),
            (p_wf4, wdtT_in, 1)]:
        sy_dma(dst[:], src, writes=[gb(dst, "pro") if tagged else b(dst)], queue="SYC")
    sy_dma(ring_f[0][:, 0:128], ident_in, writes=[gb(ring_f[0], "pro")], queue="RF0")
    dve(lambda e: e.tensor_copy(c_ident[:], ring_f[0][:, 0:128]),
        reads=[gb(ring_f[0], "pro")], writes=[b(c_ident)])
    sy_dma(ring_f[0][0:16, 0:2048], sel_in, writes=[gb(ring_f[0], "pro")], queue="RF0")
    dve(lambda e: e.tensor_copy(c_sel[:], ring_f[0][0:16, 0:2048]),
        reads=[gb(ring_f[0], "pro")], writes=[b(c_sel)])
    dve(lambda e: e.memset(c_ones_r[:], 1.0), writes=[b(c_ones_r)])
    dve(lambda e: e.memset(c_ones_c[:], 1.0), writes=[b(c_ones_c)])
    dve(lambda e: e.memset(k_eps[:], 1e-5), writes=[b(k_eps)])
    dve(lambda e: e.memset(k_half[:], 0.5), writes=[b(k_half)])
    dve(lambda e: e.memset(k_neghalf[:], -0.5), writes=[b(k_neghalf)])
    dve(lambda e: e.memset(k_invD[:], 1.0 / D), writes=[b(k_invD)])
    dve(lambda e: e.memset(k_tc[:], float(TC)), writes=[b(k_tc)])
    dve(lambda e: e.tensor_copy(wxdt_b[:], p_wf1[:]), reads=[gb(p_wf1, "pro")], writes=[b(wxdt_b)])
    dve(lambda e: e.tensor_copy(wxb_b[:], p_wf2[:]), reads=[gb(p_wf2, "pro")], writes=[b(wxb_b)])
    dve(lambda e: e.tensor_copy(wxc_b[:], p_wf3[:]), reads=[gb(p_wf3, "pro")], writes=[b(wxc_b)])
    dve(lambda e: e.tensor_copy(wdt_b[:], p_wf4[:]), reads=[gb(p_wf4, "pro")], writes=[b(wdt_b)])
    act(lambda e: e.activation(c_negA[:], c_negA[:], AF.Exp), reads=[b(c_negA)], writes=[b(c_negA)])
    dve(lambda e: e.tensor_scalar_mul(c_negA[:], c_negA[:], -1.0), reads=[b(c_negA)], writes=[b(c_negA)])
    act(lambda e: e.activation(s_t1[:], s_decay[:], AF.Sigmoid), reads=[b(s_decay)], writes=[b(s_t1)])
    act(lambda e: e.activation(s_t1[:], s_t1[:], AF.Ln), reads=[b(s_t1)], writes=[b(s_t1)])
    pe(lambda e: e.matmul(pb[4][:, 0:1], c_ones_r[:], s_t1[:], start=True, stop=True),
       reads=[b(c_ones_r), b(s_t1)], writes=[b(pb[4])])
    act(lambda e: e.copy(k_logg[:], pb[4][:, 0:1]), reads=[b(pb[4])], writes=[b(k_logg)])
    act(lambda e: e.activation(k_gC[:], k_logg[:], AF.Exp, scale=k_tc[:, 0:1]),
        reads=[b(k_logg), b(k_tc)], writes=[b(k_gC)])
    act(lambda e: e.activation(s_t1[:], s_logalpha[:], AF.Exp),
        reads=[b(s_logalpha)], writes=[b(s_t1)])
    pe(lambda e: e.matmul(pb[4][:, 0:1], c_ones_r[:], s_t1[:], start=True, stop=True),
       reads=[b(c_ones_r), b(s_t1)], writes=[b(pb[4])])
    act(lambda e: e.copy(k_alpha[:], pb[4][:, 0:1]), reads=[b(pb[4])], writes=[b(k_alpha)])
    for kk in range(4):
        act(lambda e, kk=kk: e.activation(k_M[:, kk * TC:(kk + 1) * TC],
                                          p_diffs[:, kk * TC:(kk + 1) * TC],
                                          AF.Exp, scale=k_logg[:, 0:1]),
            reads=[gb(p_diffs, "pro"), b(k_logg)], writes=[b(k_M)])
    dve(lambda e: e.tensor_mul(k_M[:], k_M[:], p_causal[:]),
        reads=[b(k_M), gb(p_causal, "pro")], writes=[b(k_M)])
    act(lambda e: e.activation(s_row[:], c_pos[:], AF.Exp, scale=k_logg[0:1, 0:1]),
        reads=[gb(c_pos, "pro"), b(k_logg)], writes=[gb(s_row, "pro")])
    pe(lambda e: e.matmul(pb[4][:, :], c_ones_r[:], s_row[:], start=True, stop=True),
       reads=[b(c_ones_r), gb(s_row, "pro")], writes=[b(pb[4])])
    act(lambda e: e.copy(k_gp[:], pb[4][:, :]), reads=[b(pb[4])], writes=[b(k_gp)])
    act(lambda e: e.activation(k_gw[:], c_revpos[:], AF.Exp, scale=k_logg[:, 0:1]),
        reads=[b(c_revpos), b(k_logg)], writes=[b(k_gw)])
    dve(lambda e: e.memset(ring_f[0][:, :], 0.0), writes=[gb(ring_f[0], "pro")])
    dve(lambda e: e.memset(ring[0][:, 0:1024], 0.0), writes=[b(ring[0])])
    for jt in range(8):
        gq_dma(scr_WT[jt], ring_f[0][:, 0:1024], reads=[gb(ring_f[0], "pro")],
               writes=[wt_bufs[jt]], queue="WTS")
        gq_dma(scr_WTb[jt], ring[0][:, 0:1024], reads=[b(ring[0])], writes=[wtb_bufs[jt]],
               queue="WTS")
    tok_st = dve(lambda e: e.memset(st_scan[:], 0.0), writes=[b(st_scan)])
    for sb_ in st_bufs:
        sb_.ready = {tok_st[0]: tok_st[1]}
    dve(lambda e: e.memset(st_ctail[:], 0.0), writes=[b(st_ctail)])
    dve(lambda e: e.memset(st_wktail[:], 0.0), writes=[b(st_wktail)])

    fam_ready = {}
    for a in range(WSPECS["w_inT"][0]):
        gq_dma(w_scr["w_inT"][a], w_f32["w_inT"][a], queue="CW%d" % (a % 2))
    fr_in = Buf()
    fr_in.ready = {"CW0": pl.count["CW0"], "CW1": pl.count["CW1"]}
    fam_ready["w_inT"] = fr_in
    rest_qs = ("CW2", "CW3", "CW4")
    ci = 0
    for nm in ["w_outT", "w1T", "w2T", "h1T", "h2T", "w3T", "w_writeT", "w_readT"]:
        ns, kt, ew = WSPECS[nm]
        for a in range(ns):
            gq_dma(w_scr[nm][a], w_f32[nm][a], queue=rest_qs[ci % 3])
            ci += 1
    fr_all = Buf()
    fr_all.ready = {q: pl.count[q] for q in rest_qs}
    for nm in ["w_outT", "w1T", "w2T", "h1T", "h2T", "w3T", "w_writeT", "w_readT"]:
        fam_ready[nm] = fr_all

    gq_dma(x0_next[:], x0_in[0].rearrange("p j t -> p (j t)"), writes=[b(x0_next)], queue="X0Q")

    rstate = [0]
    def load_wslice(nm, s, ring_set=None):
        ns, kt, ew = WSPECS[nm]
        if ring_set is None:
            si = rstate[0] % 4
        else:
            si = ring_set[rstate[0] % len(ring_set)]
        rstate[0] += 1
        slot = ring[si]
        n = kt * ew
        sy_dma(slot[:, 0:n], w_scr[nm][s].rearrange("p k e -> p (k e)"),
               reads=[fam_ready[nm]], writes=[b(slot)], queue="RS%d" % si)
        return slot

    def layernorm(xln, t1buf, tag, dest, fold=False, src_t=None):
        if src_t is None:
            src_t = x_res
        xl = gb(xln, tag) if tag else b(xln)
        t1b = gb(t1buf, tag) if tag else b(t1buf)
        for j in range(NDM):
            sl = slice(j * TC, (j + 1) * TC)
            lb = lt_bf[0]
            dve(lambda e, sl=sl, lb=lb: e.tensor_copy(lb[:], src_t[:, sl]),
                reads=[b(src_t)], writes=[b(lb)])
            pe(lambda e, lb=lb, st=(j == 0), sp=(j == NDM - 1):
               e.matmul(pb[4][0:1, :], c_ones_c[:], lb[:], start=st, stop=sp),
               reads=[b(lb), b(c_ones_c)], writes=[b(pb[4])])
        for j in range(NDM):
            sl = slice(j * TC, (j + 1) * TC)
            lb = lt_bf[0]
            act(lambda e, sl=sl, lb=lb: e.activation(lb[:], src_t[:, sl], AF.Square),
                reads=[b(src_t)], writes=[b(lb)])
            pe(lambda e, lb=lb, st=(j == 0), sp=(j == NDM - 1):
               e.matmul(pb[5][0:1, :], c_ones_c[:], lb[:], start=st, stop=sp),
               reads=[b(lb), b(c_ones_c)], writes=[b(pb[5])])
        act(lambda e: e.activation(l_m[:], pb[4][0:1, :], AF.Identity, scale=k_invD[0:1, 0:1]),
            reads=[b(pb[4]), b(k_invD)], writes=[b(l_m)])
        act(lambda e: e.activation(l_v[:], pb[5][0:1, :], AF.Identity, scale=k_invD[0:1, 0:1]),
            reads=[b(pb[5]), b(k_invD)], writes=[b(l_v)])
        act(lambda e: e.activation(l_mr[:], l_m[:], AF.Square), reads=[b(l_m)], writes=[b(l_mr)])
        dve(lambda e: e.tensor_sub(l_v[:], l_v[:], l_mr[:]), reads=[b(l_v), b(l_mr)], writes=[b(l_v)])
        act(lambda e: e.activation(l_r[:], l_v[:], AF.Ln, bias=k_eps[0:1, 0:1]),
            reads=[b(l_v), b(k_eps)], writes=[b(l_r)])
        act(lambda e: e.activation(l_r[:], l_r[:], AF.Exp, scale=k_neghalf[0:1, 0:1]),
            reads=[b(l_r), b(k_neghalf)], writes=[b(l_r)])
        dve(lambda e: e.tensor_mul(l_mr[:], l_m[:], l_r[:]), reads=[b(l_m), b(l_r)], writes=[b(l_mr)])
        pe(lambda e: e.matmul(pb[4][:, :], c_ones_r[:], l_r[:], start=True, stop=True),
           reads=[b(l_r), b(c_ones_r)], writes=[b(pb[4])])
        act(lambda e: e.copy(l_rrep[:], pb[4][:, :]), reads=[b(pb[4])], writes=[b(l_rrep)])
        pe(lambda e: e.matmul(pb[5][:, :], c_ones_r[:], l_mr[:], start=True, stop=True),
           reads=[b(l_mr), b(c_ones_r)], writes=[b(pb[5])])
        act(lambda e: e.copy(l_mrrep[:], pb[5][:, :]), reads=[b(pb[5])], writes=[b(l_mrrep)])
        for j in range(NDM):
            sl = slice(j * TC, (j + 1) * TC)
            dve(lambda e, sl=sl: e.tensor_mul(t1buf[:], src_t[:, sl], l_rrep[:]),
                reads=[b(src_t), b(l_rrep)], writes=[t1b])
            dve(lambda e, d=dest(j): e.tensor_sub(d, t1buf[:], l_mrrep[:]),
                reads=[t1b, b(l_mrrep)], writes=[xl])
            if fold:
                dve(lambda e, sl=sl: e.tensor_add(x_res[:, sl], x_res[:, sl], t1buf[:]),
                    reads=[t1b], writes=[b(x_res)])
                dve(lambda e, sl=sl: e.tensor_sub(x_res[:, sl], x_res[:, sl], l_mrrep[:]),
                    reads=[b(l_mrrep)], writes=[b(x_res)])

    def ffn(tag, adv):
        gb(f_xln, "x" + tag, G3)
        layernorm(f_xln, f_t1, tag, lambda j: f_xln[:, j * TC:(j + 1) * TC])
        fx, fh = gb(f_xln, tag), gb(f_h, tag)
        fx = gb(f_xln, "x" + tag, G3)
        fg = gb(f_g, tag)
        fg = gb(f_g, "g" + tag, G3)
        ft1 = gb(f_t1, tag)
        for s in range(8):
            slot1 = load_wslice("w1T", s)
            slot2 = load_wslice("w2T", s)
            for m in range(4):
                et = 4 * s + m
                p1, p2 = pb[(2 * m) % 6], pb[(2 * m + 1) % 6]
                mm_group(p1[:, :], b(p1),
                         [(slot1[:, k * 512 + m * 128: k * 512 + (m + 1) * 128],
                           f_xln[:, k * TC:(k + 1) * TC], [b(slot1), fx]) for k in range(NDM)])
                mm_group(p2[:, :], b(p2),
                         [(slot2[:, k * 512 + m * 128: k * 512 + (m + 1) * 128],
                           f_xln[:, k * TC:(k + 1) * TC], [b(slot2), fx]) for k in range(NDM)])
                act(lambda e, p=p1, et=et: e.activation(f_t1[:], p[:, :], AF.Silu,
                                                        scale=c_scales[:, et:et + 1]),
                    reads=[b(p1), b(c_scales)], writes=[ft1])
                dve(lambda e, p=p2, et=et: e.scalar_tensor_tensor(
                        f_h[:, et * TC:(et + 1) * TC], p[:, :],
                        c_scales[:, NH + et:NH + et + 1], f_t1[:], ALU.mult, ALU.mult),
                    reads=[b(p2), b(c_scales), ft1], writes=[fh])
            adv(1.0)
        for wnm, src, srcb, dst, dstb, so in (("h1T", f_h, fh, f_g, fg, 2 * NH),
                                              ("h2T", f_g, fg, f_h, fh, 3 * NH)):
            def gelu_evac(et, p):
                act(lambda e, p=p, et=et, so=so, dst=dst: e.activation(
                        dst[:, et * TC:(et + 1) * TC], p[:, :], AF.Gelu,
                        scale=c_scales[:, so + et:so + et + 1]),
                    reads=[b(p), b(c_scales)], writes=[dstb])
            for s in range(32):
                slot = load_wslice(wnm, s)
                p = pb[s % 6]
                mm_group(p[:, :], b(p),
                         [(slot[:, k * 128: (k + 1) * 128],
                           src[:, k * TC:(k + 1) * TC], [b(slot), srcb]) for k in range(NH)])
                if s % 2 == 1:
                    gelu_evac(s - 1, pb[(s - 1) % 6])
                    gelu_evac(s, p)
                    adv(1.0)
        for s in range(8):
            slot = load_wslice("w3T", s)
            et = s
            p = pb[s % 4]
            mm_group(p[:, :], b(p),
                     [(slot[:, k * 128: (k + 1) * 128],
                       f_h[:, k * TC:(k + 1) * TC], [b(slot), fh]) for k in range(NH)])
            dve(lambda e, p=p, et=et: e.scalar_tensor_tensor(
                    x_res[:, et * TC:(et + 1) * TC], p[:, :],
                    c_scales[:, 4 * NH + et:4 * NH + et + 1],
                    x_res[:, et * TC:(et + 1) * TC], ALU.mult, ALU.add),
                reads=[b(p), b(c_scales), b(x_res)], writes=[b(x_res)])
            adv(1.0)

    def mamba_pre(c):
        mt1 = b(m_t1)
        mpad = b(mb_pad)
        gb(m_xln, f"x{c}", G2)
        layernorm(m_xln, m_t1, None, lambda j: m_xln[:, j * TC:(j + 1) * TC], src_t=x0_next)
        mx = b(m_xln)
        for s in range(8):
            slot = load_wslice("w_inT", s)
            for m in range(4):
                et = 4 * s + m
                p = pb[et % 4]
                mm_group(p[:, :], b(p),
                         [(slot[:, k * 512 + m * 128: k * 512 + (m + 1) * 128],
                           m_xln[:, k * TC:(k + 1) * TC], [b(slot), mx]) for k in range(NDM)])
                if et < NDT:
                    i = et
                    dve(lambda e, i=i: e.tensor_copy(mb_pad[:, 0:3], st_ctail[:, i * 3: i * 3 + 3]),
                        reads=[b(st_ctail)], writes=[mpad])
                    dve(lambda e, p=p: e.tensor_scalar(mb_pad[:, 3:3 + TC], p[:, :],
                                                       0.0, 1.0, ALU.max, ALU.min),
                        reads=[b(p)], writes=[mpad])
                    dve(lambda e, i=i: e.tensor_copy(st_ctail[:, i * 3: i * 3 + 3],
                                                     mb_pad[:, TC: TC + 3]),
                        reads=[mpad], writes=[b(st_ctail)])
                    dve(lambda e, i=i: e.tensor_scalar(
                            m_t1[:], mb_pad[:, 0:TC], c_convw[:, 4 * i:4 * i + 1], None, ALU.mult),
                        reads=[mpad, b(c_convw)], writes=[mt1])
                    for q in (1, 2, 3):
                        dve(lambda e, i=i, q=q: e.scalar_tensor_tensor(
                                m_t1[:], mb_pad[:, q:q + TC],
                                c_convw[:, 4 * i + q:4 * i + q + 1], m_t1[:], ALU.mult, ALU.add),
                            reads=[mpad, b(c_convw), mt1], writes=[mt1])
                    dve(lambda e, i=i: e.tensor_scalar(
                            m_t1[:], m_t1[:], c_convb[:, i:i + 1], None, ALU.add),
                        reads=[b(c_convb), mt1], writes=[mt1])
                    act(lambda e: e.activation(lt_bf[0][:], m_t1[:], AF.Tanh, scale=k_half[:, 0:1]),
                        reads=[mt1, b(k_half)], writes=[b(lt_bf[0])])
                    dve(lambda e: e.scalar_tensor_tensor(m_t1[:], lt_bf[0][:], 1.0, m_t1[:],
                                                         ALU.add, ALU.mult),
                        reads=[mt1, b(lt_bf[0])], writes=[mt1])
                    dve(lambda e: e.tensor_scalar(m_t1[:], m_t1[:], 0.5, 0.0, ALU.mult, ALU.max),
                        reads=[mt1], writes=[mt1])
                    ev = mb_ev[i % 3]
                    dve(lambda e, ev=ev: e.tensor_scalar(ev[:], m_t1[:], 1.0, None, ALU.min),
                        reads=[mt1], writes=[b(ev)])
                    gq_dma(scr_sp["su"][i], ev[:], reads=[b(ev)], writes=[sp_bufs["su"][i]],
                           queue="SPU")
                    if YIELD_XZ:
                        yield
                else:
                    i = et - NDT
                    ev = mb_ev[i % 3]
                    dve(lambda e, p=p, ev=ev: e.tensor_scalar(ev[:], p[:, :],
                                                              0.0, 1.0, ALU.max, ALU.min),
                        reads=[b(p)], writes=[b(ev)])
                    gq_dma(scr_sp["sz"][i], ev[:], reads=[b(ev)], writes=[sp_bufs["sz"][i]],
                           queue="SPZ")
                    if YIELD_XZ:
                        yield
        uslots = []
        for g2_ in range(2):
            slot = ring[g2_]
            sy_dma(slot[:, :].rearrange("p (a t) -> p a t", a=8),
                   scr_sp["su"][8 * g2_:8 * g2_ + 8].rearrange("a p t -> p a t"),
                   reads=[sp_bufs["su"][8 * g2_ + q2] for q2 in range(8)], writes=[b(slot)],
                   queue="RS%d" % g2_)
            uslots.append(slot)
        def uview(k):
            return uslots[k // 8][:, (k % 8) * TC:(k % 8 + 1) * TC]
        mm_group(pb[0][0:64, :], b(pb[0]),
                 [(wxdt_b[:, k * DTR:(k + 1) * DTR], uview(k),
                   [b(wxdt_b), b(uslots[k // 8])]) for k in range(NDT)])
        dve(lambda e: e.tensor_scalar(m_dtraw[:], pb[0][0:64, :], 0.0, 1.0, ALU.max, ALU.min),
            reads=[b(pb[0])], writes=[b(m_dtraw)])
        mm_group(pb[1][0:16, :], b(pb[1]),
                 [(wxb_b[:, k * 16:(k + 1) * 16], uview(k),
                   [b(wxb_b), b(uslots[k // 8])]) for k in range(NDT)])
        dve(lambda e: e.tensor_copy(m_b[:], pb[1][0:16, :]), reads=[b(pb[1])], writes=[b(m_b)])
        mm_group(pb[2][0:16, :], b(pb[2]),
                 [(wxc_b[:, k * 16:(k + 1) * 16], uview(k),
                   [b(wxc_b), b(uslots[k // 8])]) for k in range(NDT)])
        dve(lambda e: e.tensor_copy(m_c[:], pb[2][0:16, :]), reads=[b(pb[2])], writes=[b(m_c)])
        if YIELD_PRE:
            yield
        for i in range(NDT):
            if YIELD_PRE and i % 2 == 0:
                yield
            p = pb[2 + i % 2]
            pe(lambda e, i=i, p=p: e.matmul(p[:, :], wdt_b[:, i * 128:(i + 1) * 128],
                                            m_dtraw[:], start=True, stop=True),
               reads=[b(wdt_b), b(m_dtraw)], writes=[b(p)])
            act(lambda e, i=i, p=p: e.activation(m_t1[:], p[:, :], AF.Exp,
                                                 bias=c_bdt[:, i:i + 1]),
                reads=[b(p), b(c_bdt)], writes=[mt1])
            act(lambda e: e.activation(m_t1[:], m_t1[:], AF.Ln, bias=1.0),
                reads=[mt1], writes=[mt1])
            ev = mb_ev[i % 3]
            dve(lambda e, ev=ev: e.tensor_scalar(ev[:], m_t1[:], 1e-4, 20.0, ALU.max, ALU.min),
                reads=[mt1], writes=[b(ev)])
            gq_dma(scr_sp["sdt"][i], ev[:], reads=[b(ev)], writes=[sp_bufs["sdt"][i]],
                   queue="SPDT")

    def scan_gen(c):
        mball = gb(m_Ball, f"bc{c}", G2)
        mcall, myacc, mytmp = b(m_Call), b(m_yacc), b(m_ytmp)
        mt2 = b(m_t2)
        def prefetch_scan(i, off, ri):
            for nm, bufs, qs in (("sdt", ms_dt, "MSD"), ("su", ms_u, "MSU"), ("sz", ms_z, "MSZ")):
                sc_dma(bufs[ri][:], scr_sp[nm][i][:, off:off + HC],
                       reads=[sp_bufs[nm][i]], writes=[b(bufs[ri])], queue=qs + str(ri))
        for hf in range(2):
            off = hf * HC
            for n in range(DS):
                pe(lambda e, n=n, off=off: e.matmul(pb[4][:, 0:HC], c_sel[:, n * 128:(n + 1) * 128],
                                           m_b[:, off:off + HC], start=True, stop=True),
                   reads=[b(c_sel), b(m_b)], writes=[b(pb[4])])
                act(lambda e, n=n: e.activation(m_Ball[:, n * HC:(n + 1) * HC],
                                                pb[4][:, 0:HC], AF.Identity),
                    reads=[b(pb[4])], writes=[mball])
                pe(lambda e, n=n, off=off: e.matmul(pb[5][:, 0:HC], c_sel[:, n * 128:(n + 1) * 128],
                                           m_c[:, off:off + HC], start=True, stop=True),
                   reads=[b(c_sel), b(m_c)], writes=[b(pb[5])])
                act(lambda e, n=n: e.activation(m_Call[:, n * HC:(n + 1) * HC],
                                                pb[5][:, 0:HC], AF.Identity),
                    reads=[b(pb[5])], writes=[mcall])
            if hf == 0 and GEN_LEVEL >= 1:
                prefetch_scan(0, 0, 0)
            for i in range(NDT):
                ri = i % 2
                sdt_b, su_b, sz_b = ms_dt[ri], ms_u[ri], ms_z[ri]
                if GEN_LEVEL >= 1:
                    if i + 1 < NDT:
                        prefetch_scan(i + 1, off, (i + 1) % 2)
                    elif hf == 0:
                        prefetch_scan(0, HC, 0)
                if GEN_LEVEL <= 1:
                    yield
                    continue
                dtu = m_dtu2[i % 2]
                dve(lambda e, dtu=dtu, sdt_b=sdt_b, su_b=su_b: e.tensor_mul(
                        dtu[:], sdt_b[:], su_b[:]),
                    reads=[b(sdt_b), b(su_b)], writes=[b(dtu)])
                def emit_dA(n):
                    g = i * DS + n
                    dA = m_dA[g % 8]
                    act(lambda e, g=g, dA=dA, sdt_b=sdt_b: e.activation(
                            dA[:], sdt_b[:], AF.Exp, scale=c_negA[:, g:g + 1]),
                        reads=[b(sdt_b), b(c_negA)], writes=[b(dA)])
                def emit_dBu(n):
                    g = i * DS + n
                    dBu = m_dBu[g % 3]
                    dve(lambda e, n=n, dBu=dBu, dtu=dtu: e.tensor_mul(
                            dBu[:], dtu[:], m_Ball[:, n * HC:(n + 1) * HC]),
                        reads=[b(dtu), mball], writes=[b(dBu)])
                emit_dBu(0)
                for n in range(DS):
                    g = i * DS + n
                    dA = m_dA[g % 8]
                    dBu = m_dBu[g % 3]
                    hh = m_h[g % 3]
                    if n % 8 == 0:
                        for q in range(8):
                            emit_dA(n + q)
                    if SCAN_AS_MUL:
                        dve(lambda e, g=g, dA=dA, dBu=dBu, hh=hh: e.tensor_mul(
                                hh[:], dA[:], dBu[:]),
                            reads=[b(dA), b(dBu), st_bufs[g]], writes=[b(hh)])
                    else:
                        dve(lambda e, g=g, dA=dA, dBu=dBu, hh=hh: e.tensor_tensor_scan(
                                hh[:], dA[:], dBu[:], st_scan[:, g:g + 1], ALU.mult, ALU.add),
                            reads=[b(dA), b(dBu), st_bufs[g]], writes=[b(hh)])
                    if n + 1 < DS:
                        emit_dBu(n + 1)
                    pool(lambda e, g=g, hh=hh: e.tensor_copy(st_scan[:, g:g + 1],
                                                             hh[:, HC - 1:HC]),
                         reads=[b(hh)], writes=[st_bufs[g]])
                    if n == 0:
                        dve(lambda e, n=n, hh=hh: e.tensor_mul(
                                m_yacc[:], hh[:], m_Call[:, n * HC:(n + 1) * HC]),
                            reads=[b(hh), mcall], writes=[myacc])
                    else:
                        dve(lambda e, n=n, hh=hh: e.tensor_mul(
                                m_ytmp[:], hh[:], m_Call[:, n * HC:(n + 1) * HC]),
                            reads=[b(hh), mcall], writes=[mytmp])
                        dve(lambda e: e.tensor_add(m_yacc[:], m_yacc[:], m_ytmp[:]),
                            reads=[mytmp], writes=[myacc])
                    if n % 8 == 7 and n + 1 < DS:
                        yield
                dve(lambda e, su_b=su_b, i=i: e.scalar_tensor_tensor(
                        m_yacc[:], su_b[:], c_D[:, i:i + 1], m_yacc[:], ALU.mult, ALU.add),
                    reads=[b(su_b), b(c_D)], writes=[myacc])
                dve(lambda e: e.tensor_scalar(m_ytmp[:], m_yacc[:], 0.0, 1.0, ALU.max, ALU.min),
                    reads=[myacc], writes=[mytmp])
                act(lambda e, sz_b=sz_b: e.activation(m_t2[:, 0:HC], sz_b[:],
                                                      AF.Tanh, scale=k_half[:, 0:1]),
                    reads=[b(sz_b), b(k_half)], writes=[mt2])
                dve(lambda e, sz_b=sz_b: e.scalar_tensor_tensor(
                        m_t2[:, 0:HC], m_t2[:, 0:HC], 1.0, sz_b[:], ALU.add, ALU.mult),
                    reads=[mt2, b(sz_b)], writes=[mt2])
                dve(lambda e: e.tensor_mul(m_ytmp[:], m_ytmp[:], m_t2[:, 0:HC]),
                    reads=[mt2], writes=[mytmp])
                yb = ms_y[i % 2]
                dve(lambda e, yb=yb: e.tensor_scalar(yb[:], m_ytmp[:],
                                                     0.5, 1.0, ALU.mult, ALU.min),
                    reads=[mytmp], writes=[b(yb)])
                gq_dma(scr_sp["sy"][i][:, off:off + HC], yb[:], reads=[b(yb)],
                       writes=[sp_bufs["sy"][i]], queue="SPY")
                yield

    def wout(c):
        yslots = []
        for g2_ in range(2):
            slot = ring[g2_]
            sy_dma(slot[:, :].rearrange("p (a t) -> p a t", a=8),
                   scr_sp["sy"][8 * g2_:8 * g2_ + 8].rearrange("a p t -> p a t"),
                   reads=[sp_bufs["sy"][8 * g2_ + q2] for q2 in range(8)], writes=[b(slot)],
                   queue="RS%d" % g2_)
            yslots.append(slot)
        def yview(k):
            return yslots[k // 8][:, (k % 8) * TC:(k % 8 + 1) * TC]
        for s in range(4):
            slot = load_wslice("w_outT", s, ring_set=(2, 3))
            for m in range(2):
                et = 2 * s + m
                p = pb[et % 4]
                mm_group(p[:, :], b(p),
                         [(slot[:, k * 256 + m * 128: k * 256 + (m + 1) * 128],
                           yview(k), [b(slot), b(yslots[k // 8])]) for k in range(NDT)])
                dve(lambda e, p=p, et=et: e.tensor_add(x_res[:, et * TC:(et + 1) * TC],
                                                       p[:, :], x0_next[:, et * TC:(et + 1) * TC]),
                    reads=[b(p), b(x0_next)], writes=[b(x_res)])
        if c + 1 < nchunk:
            gq_dma(x0_next[:], x0_in[c + 1].rearrange("p j t -> p (j t)"), writes=[b(x0_next)],
                   queue="X0Q")

    def attnb_fn(c, adv):
        at = f"a{c}"
        rkp = gb(a_rkpad, at)
        for j in range(NDM):
            dve(lambda e, j=j: e.tensor_copy(a_rkpad[:, j * 513: j * 513 + 1],
                                             st_wktail[:, j:j + 1]),
                reads=[b(st_wktail)], writes=[rkp])
        layernorm(a_rkpad, a_t1, at, lambda j: a_rkpad[:, j * 513 + 1: j * 513 + 513], fold=True)
        for j in range(NDM):
            dve(lambda e, j=j: e.tensor_copy(st_wktail[:, j:j + 1],
                                             a_rkpad[:, j * 513 + 512: j * 513 + 513]),
                reads=[rkp], writes=[b(st_wktail)])
        adv(1.3)
        for eh in range(2):
            slot = load_wslice("w_writeT", eh, ring_set=(2, 3))
            for ct in range(4):
                p = pb[ct % 2]
                mm_group(p[:, :], b(p),
                         [(a_rkpad[:, k * 513 + 1 + ct * 128: k * 513 + 1 + (ct + 1) * 128],
                           slot[:, k * 512:(k + 1) * 512], [rkp, b(slot)]) for k in range(NDM)])
                dve(lambda e, p=p, ct=ct, eh=eh: e.tensor_copy(
                        a_vtm[:, ct * 1024 + eh * 512: ct * 1024 + (eh + 1) * 512], p[:, :]),
                    reads=[b(p)], writes=[gb(a_vtm, at)])
            adv(1.3)
        for kt in range(4):
            p = pb[2 + kt % 2]
            mm_group(p[:, :], b(p),
                     [(a_rkpad[:, k * 513 + kt * 128: k * 513 + (kt + 1) * 128],
                       a_rkpad[:, k * 513 + 1: k * 513 + 513], [rkp]) for k in range(NDM)])
            dve(lambda e, p=p, kt=kt: e.tensor_mul(a_SmT[:, kt * TC:(kt + 1) * TC],
                                                   p[:, :], k_M[:, kt * TC:(kt + 1) * TC]),
                reads=[b(p), b(k_M)], writes=[gb(a_SmT, at)])
        adv(1.3)
        for ct in range(4):
            for jd in range(NDM):
                pe(lambda e, ct=ct, jd=jd: e.transpose(
                        pbT[:, :], a_rkpad[:, jd * 513 + ct * 128: jd * 513 + (ct + 1) * 128],
                        c_ident[:]),
                   reads=[rkp, b(c_ident)], writes=[b(pbT)])
                dve(lambda e, ct=ct, jd=jd: e.tensor_scalar(
                        a_wktm[:, ct * 1024 + jd * 128: ct * 1024 + (jd + 1) * 128], pbT[:, :],
                        k_gw[:, ct:ct + 1], None, ALU.mult),
                    reads=[b(pbT), b(k_gw)], writes=[gb(a_wktm, at)])
            adv(1.3)
        for j in range(NDM):
            dve(lambda e, j=j: e.tensor_mul(a_rkpad[:, j * 513 + 1: j * 513 + 513],
                                            a_rkpad[:, j * 513 + 1: j * 513 + 513], k_gp[:]),
                reads=[b(k_gp), rkp], writes=[rkp])
        wtb_slots = []
        for g2_ in range(2):
            slot = ring[g2_]
            sy_dma(slot[:, :].rearrange("p (a t) -> p a t", a=4),
                   scr_WTb[4 * g2_:4 * g2_ + 4].rearrange("a p t -> p a t"),
                   reads=[wtb_bufs[4 * g2_ + q2] for q2 in range(4)], writes=[b(slot)],
                   queue="RS%d" % g2_)
            wtb_slots.append(slot)
        def wtbview(jt, i):
            return wtb_slots[jt // 4][:, (jt % 4) * 1024 + i * 128: (jt % 4) * 1024 + (i + 1) * 128]
        for i in range(NDM):
            p = pb[i % 2]
            pairs = [(wtbview(jt, i),
                      a_rkpad[:, jt * 513 + 1: jt * 513 + 513], [b(wtb_slots[jt // 4]), rkp])
                     for jt in range(8)]
            pairs += [(a_vtm[:, ct * 1024 + i * 128: ct * 1024 + (i + 1) * 128],
                       a_SmT[:, ct * TC:(ct + 1) * TC], [gb(a_vtm, at), gb(a_SmT, at)])
                      for ct in range(4)]
            mm_group(p[:, :], b(p), pairs)
            dve(lambda e, p=p, i=i: e.tensor_copy(a_reads[:, i * TC:(i + 1) * TC], p[:, :]),
                reads=[b(p)], writes=[gb(a_reads, at)])
            if i % 2 == 1:
                adv(1.3)
        for eh in range(2):
            slot = load_wslice("w_readT", eh, ring_set=(2, 3))
            for e2 in range(4):
                g2 = 4 * eh + e2
                p = pb[2 + e2 % 2]
                mm_group(p[:, :], b(p),
                         [(slot[:, k * 512 + e2 * 128: k * 512 + (e2 + 1) * 128],
                           a_reads[:, k * TC:(k + 1) * TC], [b(slot), gb(a_reads, at)])
                          for k in range(NDM)])
                dve(lambda e, p=p, g2=g2: e.scalar_tensor_tensor(
                        x_res[:, g2 * TC:(g2 + 1) * TC], p[:, :], k_alpha[:, 0:1],
                        x_res[:, g2 * TC:(g2 + 1) * TC], ALU.mult, ALU.add),
                    reads=[b(p), b(k_alpha), b(x_res)], writes=[b(x_res)])
            adv(1.3)
        for jt in range(8):
            awf = aw_f32[jt % 2]
            awb = gb(awf, at)
            abb = gb(aw_bf, at)
            sc_dma(awf[:], scr_WT[jt], reads=[wt_bufs[jt]], writes=[awb],
                   queue="AW%d" % (jt % 2))
            for ih in range(2):
                mm_group(pb_dw[:, :], b(pb_dw),
                         [(a_wktm[:, ct * 1024 + jt * 128: ct * 1024 + (jt + 1) * 128],
                           a_vtm[:, ct * 1024 + ih * 512: ct * 1024 + (ih + 1) * 512],
                           [gb(a_wktm, at), gb(a_vtm, at)]) for ct in range(4)])
                dve(lambda e, awf=awf, ih=ih: e.scalar_tensor_tensor(
                        awf[:, ih * 512:(ih + 1) * 512], awf[:, ih * 512:(ih + 1) * 512],
                        k_gC[:, 0:1], pb_dw[:, :], ALU.mult, ALU.add),
                    reads=[awb, b(k_gC), b(pb_dw)], writes=[awb])
            dve(lambda e, awf=awf: e.tensor_copy(aw_bf[:], awf[:]),
                reads=[awb], writes=[abb])
            gq_dma(scr_WT[jt], awf[:], reads=[awb], writes=[wt_bufs[jt]], queue="WTS")
            gq_dma(scr_WTb[jt], aw_bf[:], reads=[abb], writes=[wtb_bufs[jt]], queue="WTS")
            if jt % 2 == 1:
                adv(1.3)

    gen_holder = []
    adv_acc = [0.0]
    def adv(w=1.0):
        adv_acc[0] += w
        k = int(adv_acc[0])
        adv_acc[0] -= k
        for _ in range(k):
            if not gen_holder:
                break
            if next(gen_holder[0], "END") == "END":
                gen_holder.pop(0)
    def drain():
        while gen_holder:
            if next(gen_holder[0], "END") == "END":
                gen_holder.pop(0)

    for c in range(nchunk):
        if c == 0:
            gen_holder.append(mamba_pre(0))
            gen_holder.append(scan_gen(0))
            drain()
        wout(c)
        if dbg == 1:
            gq_dma(dbg_o["dbg_x1"][c], x_res[:], reads=[b(x_res)], queue="STQ")
        if c + 1 < nchunk:
            gen_holder.append(mamba_pre(c + 1))
            gen_holder.append(scan_gen(c + 1))
            if DRAIN_STAGE == 0:
                drain()
        ffn(f"f{c}a", adv)
        if DRAIN_STAGE == 1:
            drain()
        if dbg == 1:
            gq_dma(dbg_o["dbg_x2"][c], x_res[:], reads=[b(x_res)], queue="STQ")
        attnb_fn(c, adv)
        if DRAIN_STAGE == 2:
            drain()
        if dbg == 1:
            gq_dma(dbg_o["dbg_x3"][c], x_res[:], reads=[b(x_res)], queue="STQ")
        ffn(f"f{c}b", adv)
        drain()
        gq_dma(out_fm[c], x_res[:].rearrange("p (j t) -> p j t", j=NDM),
               reads=[b(x_res)], queue="STQ")

    if dbg == 2:
        for nm, scr in (("dbg_sdt", scr_sp["sdt"]), ("dbg_su", scr_sp["su"]),
                        ("dbg_sz", scr_sp["sz"]), ("dbg_sy", scr_sp["sy"])):
            gq_dma(dbg_o[nm].rearrange("a p t -> p a t"), scr.rearrange("a p t -> p a t"),
                   reads=[sp_bufs[nm[4:]][q2] for q2 in range(NDT)], queue="STQ")
        gq_dma(dbg_o["dbg_wtb"].rearrange("a p t -> p a t"), scr_WTb.rearrange("a p t -> p a t"),
               reads=wtb_bufs, queue="STQ")
        gq_dma(dbg_o["dbg_wtf"].rearrange("a p t -> p a t"), scr_WT.rearrange("a p t -> p a t"),
               reads=wt_bufs, queue="STQ")
    pl.op("gpsimd", lambda e: e.engine_nop(), extra=[("STQ", pl.count["STQ"])])

    with nc.Block() as block:
        def runner(name):
            own = ENG_SEMS[name]
            skip_self = name == "tensor"
            def run(eng):
                last = {}
                for waits, emit, inc in pl.ops[name]:
                    for s in sorted(waits):
                        v = waits[s]
                        if skip_self and s == own:
                            continue
                        if last.get(s, -1) < v:
                            eng.wait_ge(sems[s], v)
                            last[s] = v
                    inst = emit(eng)
                    if inc is not None:
                        inst.then_inc(sems[inc[0]], inc[1])
            return run
        block.sync(runner("sync"))
        block.tensor(runner("tensor"))
        block.vector(runner("vector"))
        block.scalar(runner("scalar"))
        block.gpsimd(runner("gpsimd"))

    es.close()
    return nc


def _tile_w(WT2, ew):
    K, M = WT2.shape
    kt, ns = K // 128, M // ew
    return np.ascontiguousarray(WT2.reshape(kt, 128, ns, ew).transpose(2, 1, 0, 3))


def prep_inputs(inputs, nchunk=NCHUNK):
    f32 = lambda x: np.ascontiguousarray(np.asarray(x, np.float32))
    W_in = f32(inputs["L0_attn_a_W_in"])
    conv_w = f32(inputs["L0_attn_a_conv_w"]).reshape(DI, 4)
    conv_b = f32(inputs["L0_attn_a_conv_b"])
    W_x = f32(inputs["L0_attn_a_W_x"])
    W_dt = f32(inputs["L0_attn_a_W_dt"])
    b_dt = f32(inputs["L0_attn_a_b_dt"])
    W_out = f32(inputs["L0_attn_a_W_out"])
    A_log = f32(inputs["L0_attn_a_A_log"])
    D_skip = f32(inputs["L0_attn_a_D"])
    W_write = f32(inputs["L0_attn_b_W_write"])
    W_read = f32(inputs["L0_attn_b_W_read"])
    w1_w = f32(inputs["L0_ffn_w1_w"]); w1_s = f32(inputs["L0_ffn_w1_s"])
    w2_w = f32(inputs["L0_ffn_w2_w"]); w2_s = f32(inputs["L0_ffn_w2_s"])
    h1_w = f32(inputs["L0_ffn_h1_w"]); h1_s = f32(inputs["L0_ffn_h1_s"])
    h2_w = f32(inputs["L0_ffn_h2_w"]); h2_s = f32(inputs["L0_ffn_h2_s"])
    w3_w = f32(inputs["L0_ffn_w3_w"]); w3_s = f32(inputs["L0_ffn_w3_s"])

    com = {}
    com["w_inT"] = _tile_w(W_in.T, 512)
    com["w_outT"] = _tile_w(W_out.T, 256)
    com["w_writeT"] = _tile_w(W_write.T, 512)
    com["w_readT"] = _tile_w(W_read.T, 512)
    com["w1T"] = _tile_w(w1_w.T, 512)
    com["w2T"] = _tile_w(w2_w.T, 512)
    com["h1T"] = _tile_w(h1_w.T, 128)
    com["h2T"] = _tile_w(h2_w.T, 128)
    com["w3T"] = _tile_w(w3_w.T, 128)
    def kp(W2, m):
        K = W2.shape[0]
        kt = K // 128
        return np.ascontiguousarray(W2.reshape(kt, 128, m).transpose(1, 0, 2).reshape(128, kt * m))
    com["wx_dtT"] = kp(W_x[0:DTR].T, DTR)
    com["wx_bT"] = kp(W_x[DTR:DTR + 16].T, 16)
    com["wx_cT"] = kp(W_x[DTR + 16:DTR + 32].T, 16)
    com["wdtT"] = np.ascontiguousarray(W_dt.T)
    com["conv_w_l"] = np.ascontiguousarray(conv_w.reshape(NDT, 128, 4).transpose(1, 0, 2).reshape(128, NDT * 4))
    com["conv_b_l"] = np.ascontiguousarray(conv_b.reshape(NDT, 128).T)
    com["D_l"] = np.ascontiguousarray(D_skip.reshape(NDT, 128).T)
    com["bdt_l"] = np.ascontiguousarray(b_dt.reshape(NDT, 128).T)
    com["Alog_l"] = np.ascontiguousarray(A_log.reshape(NDT, 128, DS).transpose(1, 0, 2).reshape(128, NDT * DS))
    com["scales_l"] = np.ascontiguousarray(np.concatenate([
        w1_s.reshape(NH, 128).T, w2_s.reshape(NH, 128).T, h1_s.reshape(NH, 128).T,
        h2_s.reshape(NH, 128).T, w3_s.reshape(NDM, 128).T], axis=1))
    com["sel16"] = np.repeat(np.eye(16, dtype=np.float32)[:, :, None], 128, 2).reshape(16, 2048)
    kk = np.arange(TC)
    karr = (np.arange(4)[:, None, None] * 128 + np.arange(128)[None, :, None])
    cgrid = kk[None, None, :]
    diffs = np.maximum(cgrid - 1 - karr, 0).astype(np.float32)
    causal = (cgrid > karr).astype(np.float32)
    com["diffs_l"] = np.ascontiguousarray(diffs.transpose(1, 0, 2).reshape(128, 4 * TC))
    com["causal_l"] = np.ascontiguousarray(causal.transpose(1, 0, 2).reshape(128, 4 * TC))
    com["pos_row"] = np.arange(TC, dtype=np.float32).reshape(1, TC)
    com["revpos_l"] = np.ascontiguousarray(
        (TC - 1 - (np.arange(4)[:, None] * 128 + np.arange(128)[None, :])).astype(np.float32).T)
    com["ident"] = np.eye(128, dtype=np.float32)
    com["decay_s"] = f32(inputs["L0_attn_b_decay"]).reshape(1, 1)
    com["logalpha_s"] = f32(inputs["L0_attn_b_log_alpha"]).reshape(1, 1)

    hs = f32(inputs["L0_attn_a_hidden_states"])
    x0s = []
    for bb in range(NCORE):
        xfm = hs[bb].T
        x0 = np.ascontiguousarray(
            xfm.reshape(NDM, 128, NCHUNK, TC).transpose(2, 1, 0, 3))[:nchunk]
        x0s.append(x0)
    return com, x0s


def kernel(**inputs):
    com, x0s = prep_inputs(inputs)
    nc = build_program()
    in_maps = [dict(com, x0=x0s[bb]) for bb in range(NCORE)]
    res = run_bass_kernel_spmd(nc, in_maps, core_ids=list(range(NCORE)))
    out = np.empty((NCORE, T, D), np.float32)
    for bb in range(NCORE):
        o = np.asarray(res.results[bb]["out_fm"], np.float32)
        xfm = o.transpose(2, 1, 0, 3).reshape(D, T)
        out[bb] = xfm.T
    return out


if __name__ == "__main__":
    pass


# version 17
# speedup vs baseline: 1.0282x; 1.0282x over previous
import sys
sys.path.insert(0, "/opt/trn_rl_repo")
import contextlib
import os
import numpy as np
import ml_dtypes

import concourse.bass as bass
import concourse.mybir as mybir
from concourse.bass_utils import run_bass_kernel_spmd

F32 = mybir.dt.float32
BF16 = mybir.dt.bfloat16
AF = mybir.ActivationFunctionType
ALU = mybir.AluOpType

NCORE = 8
T = 8192
D = 1024
DI = 2048
DS = 16
DTR = 64
TC = 512
HC = 256
NCHUNK = T // TC
NDT = DI // 128
NDM = D // 128
DH = 4096
NH = DH // 128

ENG_SEMS = {"sync": "SY", "tensor": "PE", "vector": "DV", "scalar": "AC", "gpsimd": "PL"}

WSPECS = {
    "w_inT":    (8, 8, 512),
    "w_outT":   (4, 16, 256),
    "w_writeT": (2, 8, 512),
    "w_readT":  (2, 8, 512),
    "w1T":      (8, 8, 512),
    "w2T":      (8, 8, 512),
    "h1T":      (32, 32, 128),
    "h2T":      (32, 32, 128),
    "w3T":      (8, 32, 128),
}


class Buf:
    __slots__ = ("ready", "users", "groups")
    def __init__(self):
        self.ready = {}
        self.users = {}
        self.groups = []


class Group:
    def __init__(self):
        self.tag = None
        self.users = {}


DMA_SEMS = (["RS0", "RS1", "RS2", "RS3", "RF0", "RF1",
             "MSD0", "MSD1", "MSU0", "MSU1", "MSZ0", "MSZ1",
             "SPU", "SPZ", "SPDT", "SPY", "AW0", "AW1", "WTS",
             "SCW", "SYC", "X0Q", "STQ", "SYQ", "GQQ",
             "CW0", "CW1", "CW2", "CW3", "CW4"])
FULL_WAIT = {"SPU", "SPZ", "SPDT", "SPY", "WTS", "SCW", "SYC", "X0Q"}


class Planner:
    def __init__(self):
        self.ops = {e: [] for e in ENG_SEMS}
        self.count = {s: 0 for s in list(ENG_SEMS.values()) + DMA_SEMS}

    def op(self, engine, emit, reads=(), writes=(), extra=(), dma_queue=None, defer=False):
        waits = {}
        def mw(s, v):
            if v > waits.get(s, -1):
                waits[s] = v
        switched = []
        for blist, iswrite in ((reads, False), (writes, True)):
            for bb in blist:
                if iswrite:
                    for s, v in bb.users.items():
                        mw(s, v)
                for s, v in bb.ready.items():
                    mw(s, v)
                for gr, tag in bb.groups:
                    if gr.tag != tag:
                        switched.append((gr, tag))
        for gr, tag in switched:
            for s, v in gr.users.items():
                mw(s, v)
            gr.users = {}
            gr.tag = tag
        for t_ in extra:
            if t_ is not None:
                mw(t_[0], t_[1])
        for s in list(waits):
            if s in FULL_WAIT:
                waits[s] = self.count[s]
        if dma_queue is not None:
            sem = dma_queue
            self.count[sem] += 16
            tok = (sem, self.count[sem])
            inc = (sem, 16)
        elif defer:
            sem = ENG_SEMS[engine]
            tok = (sem, self.count[sem] + 1)
            inc = None
        else:
            sem = ENG_SEMS[engine]
            self.count[sem] += 1
            tok = (sem, self.count[sem])
            inc = (sem, 1)
        self.ops[engine].append((waits, emit, inc))
        for bb in reads:
            if tok[1] > bb.users.get(tok[0], -1):
                bb.users[tok[0]] = tok[1]
            for gr, _t in bb.groups:
                if tok[1] > gr.users.get(tok[0], -1):
                    gr.users[tok[0]] = tok[1]
        for bb in writes:
            bb.ready = {tok[0]: tok[1]}
            bb.users = {tok[0]: tok[1]}
            for gr, _t in bb.groups:
                if tok[1] > gr.users.get(tok[0], -1):
                    gr.users[tok[0]] = tok[1]
        return tok


NO_SCAN = False
SCAN_AS_MUL = False
GEN_LEVEL = 9
YIELD_XZ = False
YIELD_PRE = False
DRAIN_STAGE = 3
RACE_DET = True

def build_program(nchunk=NCHUNK, dbg=False):
    nc = bass.Bass("TRN2", target_bir_lowering=False, debug=False, detect_race_conditions=RACE_DET)
    es = contextlib.ExitStack()

    def din(name, shape, dt=F32):
        return nc.dram_tensor(name, shape, dt, kind="ExternalInput").ap()

    x0_in = din("x0", [nchunk, 128, NDM, TC])
    out_fm = nc.dram_tensor("out_fm", [nchunk, 128, NDM, TC], F32, kind="ExternalOutput").ap()

    w_f32, w_scr = {}, {}
    for nm, (ns, kt, ew) in WSPECS.items():
        w_f32[nm] = din(nm, [ns, 128, kt, ew])
        w_scr[nm] = nc.dram_tensor("scr_" + nm, [ns, 128, kt, ew], BF16, kind="Internal").ap()

    scr_sp = {}
    for nm in ("sdt", "su", "sz", "sy"):
        scr_sp[nm] = nc.dram_tensor("scr_" + nm, [NDT, 128, TC], BF16, kind="Internal").ap()
    scr_WT = nc.dram_tensor("scr_WT", [8, 128, 1024], F32, kind="Internal").ap()
    scr_WTb = nc.dram_tensor("scr_WTb", [8, 128, 1024], BF16, kind="Internal").ap()

    wxdt_in = din("wx_dtT", [128, NDT * DTR])
    wxb_in = din("wx_bT", [128, NDT * 16])
    wxc_in = din("wx_cT", [128, NDT * 16])
    wdtT_in = din("wdtT", [64, DI])
    convw_in = din("conv_w_l", [128, NDT * 4])
    convb_in = din("conv_b_l", [128, NDT])
    D_in = din("D_l", [128, NDT])
    bdt_in = din("bdt_l", [128, NDT])
    Alog_in = din("Alog_l", [128, NDT * DS])
    scales_in = din("scales_l", [128, 4 * NH + NDM])
    sel_in = din("sel16", [16, 16 * 128])
    diffs_in = din("diffs_l", [128, 4 * TC])
    causal_in = din("causal_l", [128, 4 * TC])
    pos_in = din("pos_row", [1, TC])
    revpos_in = din("revpos_l", [128, 4])
    ident_in = din("ident", [128, 128])
    decay_in = din("decay_s", [1, 1])
    logalpha_in = din("logalpha_s", [1, 1])

    dbg_o = {}
    if dbg == 1:
        for nm in ("dbg_x1", "dbg_x2", "dbg_x3"):
            dbg_o[nm] = nc.dram_tensor(nm, [nchunk, 128, NDM * TC], F32, kind="ExternalOutput").ap()
    if dbg == 2:
        for nm in ("dbg_sdt", "dbg_su", "dbg_sz", "dbg_sy"):
            dbg_o[nm] = nc.dram_tensor(nm, [NDT, 128, TC], BF16, kind="ExternalOutput").ap()
        dbg_o["dbg_wtb"] = nc.dram_tensor("dbg_wtb", [8, 128, 1024], BF16, kind="ExternalOutput").ap()
        dbg_o["dbg_wtf"] = nc.dram_tensor("dbg_wtf", [8, 128, 1024], F32, kind="ExternalOutput").ap()

    def salloc(name, shape, dt):
        return nc.alloc_sbuf_tensor(name, list(shape), dt)

    x_res = salloc("x_res", [128, NDM * TC], F32)
    x0_next = salloc("x0_next", [128, NDM * TC], F32)
    st_scan = salloc("st_scan", [128, NDT * DS], F32)
    st_ctail = salloc("st_ctail", [128, NDT * 3], BF16)
    st_wktail = salloc("st_wktail", [128, NDM], BF16)
    ring = [salloc(f"ring{i}", [128, 4096], BF16) for i in range(4)]
    k_M = salloc("k_M", [128, 4 * TC], BF16)
    k_gp = salloc("k_gp", [128, TC], BF16)
    k_gw = salloc("k_gw", [128, 4], F32)
    k_logg = salloc("k_logg", [128, 1], F32)
    k_gC = salloc("k_gC", [128, 1], F32)
    k_alpha = salloc("k_alpha", [128, 1], F32)
    c_convw = salloc("c_convw", [128, NDT * 4], F32)
    c_convb = salloc("c_convb", [128, NDT], F32)
    c_D = salloc("c_D", [128, NDT], F32)
    c_bdt = salloc("c_bdt", [128, NDT], F32)
    c_negA = salloc("c_negA", [128, NDT * DS], F32)
    c_scales = salloc("c_scales", [128, 4 * NH + NDM], F32)
    c_sel = salloc("c_sel", [16, 16 * 128], BF16)
    c_revpos = salloc("c_revpos", [128, 4], F32)
    c_ident = salloc("c_ident", [128, 128], BF16)
    c_ones_r = salloc("c_ones_r", [1, 128], F32)
    c_ones_c = salloc("c_ones_c", [128, 1], BF16)
    s_decay = salloc("s_decay", [1, 1], F32)
    s_logalpha = salloc("s_logalpha", [1, 1], F32)
    s_t1 = salloc("s_t1", [1, 1], F32)
    k_eps = salloc("k_eps", [128, 1], F32)
    k_half = salloc("k_half", [128, 1], F32)
    k_neghalf = salloc("k_neghalf", [128, 1], F32)
    k_invD = salloc("k_invD", [128, 1], F32)
    k_tc = salloc("k_tc", [128, 1], F32)
    l_m = salloc("l_m", [1, TC], F32)
    l_v = salloc("l_v", [1, TC], F32)
    l_r = salloc("l_r", [1, TC], F32)
    l_mr = salloc("l_mr", [1, TC], F32)
    l_rrep = salloc("l_rrep", [128, TC], F32)
    l_mrrep = salloc("l_mrrep", [128, TC], F32)
    lt_bf = [salloc("lt_bf0", [128, TC], BF16)]
    wxdt_b = salloc("wxdt_b", [128, NDT * DTR], BF16)
    wxb_b = salloc("wxb_b", [128, NDT * 16], BF16)
    wxc_b = salloc("wxc_b", [128, NDT * 16], BF16)
    wdt_b = salloc("wdt_b", [64, DI], BF16)

    ARENA_BYTES = (nc.sbuf_bytes_remaining - 256) // 32 * 32
    arena_h = nc.alloc_sbuf_tensor("arena", [128, ARENA_BYTES], mybir.dt.uint8)
    AB = nc.lookup_mloc(arena_h).addr
    assert AB is not None

    def aalloc(cur2, name, shape, dt):
        per = int(np.prod(shape[1:])) * mybir.dt.size(dt)
        off = (cur2[0] + 31) // 32 * 32
        cur2[0] = off + per
        assert cur2[0] <= ARENA_BYTES, f"arena overflow {name}: {cur2[0]} > {ARENA_BYTES}"
        return nc.alloc_sbuf_tensor_at(name, list(shape), dt, offset=AB + off)

    am = [0]
    mb_pad = aalloc(am, "mb_pad", [128, TC + 3], BF16)
    mb_ev = [aalloc(am, f"mb_ev{i}", [128, TC], BF16) for i in range(3)]
    ms_dt = [aalloc(am, f"ms_dt{i}", [128, HC], BF16) for i in range(2)]
    ms_u = [aalloc(am, f"ms_u{i}", [128, HC], BF16) for i in range(2)]
    ms_z = [aalloc(am, f"ms_z{i}", [128, HC], BF16) for i in range(2)]
    ms_y = [aalloc(am, f"ms_y{i}", [128, HC], BF16) for i in range(2)]
    m_dtraw = aalloc(am, "m_dtraw", [64, TC], BF16)
    m_b = aalloc(am, "m_b", [16, TC], BF16)
    m_c = aalloc(am, "m_c", [16, TC], BF16)
    xb_off = (am[0] + 31) // 32 * 32
    m_xln = nc.alloc_sbuf_tensor_at("m_xln", [128, NDM * TC], BF16, offset=AB + xb_off)
    m_Ball = nc.alloc_sbuf_tensor_at("m_Ball", [128, DS * HC], BF16, offset=AB + xb_off)
    am[0] = xb_off + DS * HC * 2
    m_Call = aalloc(am, "m_Call", [128, DS * HC], BF16)
    m_dtu2 = [aalloc(am, f"m_dtu2{i}", [128, HC], BF16) for i in range(2)]
    m_dA = [aalloc(am, f"m_dA{i}", [128, HC], F32) for i in range(8)]
    m_dBu = [aalloc(am, f"m_dBu{i}", [128, HC], BF16) for i in range(3)]
    m_h = [aalloc(am, f"m_h{i}", [128, HC], BF16) for i in range(3)]
    m_yacc = aalloc(am, "m_yacc", [128, HC], F32)
    m_ytmp = aalloc(am, "m_ytmp", [128, HC], BF16)
    m_t1 = aalloc(am, "m_t1", [128, TC], F32)
    m_t2 = aalloc(am, "m_t2", [128, HC], BF16)
    MS = (am[0] + 31) // 32 * 32

    ap_ = [MS]
    ring_f = [aalloc(ap_, "ringf0", [128, 2048], F32)]
    p_diffs = aalloc(ap_, "p_diffs", [128, 4 * TC], F32)
    p_causal = aalloc(ap_, "p_causal", [128, 4 * TC], F32)
    p_wf1 = aalloc(ap_, "p_wf1", [128, NDT * DTR], F32)
    p_wf2 = aalloc(ap_, "p_wf2", [128, NDT * 16], F32)
    p_wf3 = aalloc(ap_, "p_wf3", [128, NDT * 16], F32)
    p_wf4 = aalloc(ap_, "p_wf4", [64, DI], F32)
    c_pos = aalloc(ap_, "c_pos", [1, TC], F32)
    s_row = aalloc(ap_, "s_row", [1, TC], F32)

    af = [MS]
    f_h = aalloc(af, "f_h", [128, NH * TC], BF16)
    fg_off = (af[0] + 31) // 32 * 32
    f_g = nc.alloc_sbuf_tensor_at("f_g", [128, NH * TC], BF16, offset=AB + fg_off)
    f_xln = nc.alloc_sbuf_tensor_at("f_xln", [128, NDM * TC], BF16, offset=AB + fg_off)
    af[0] = fg_off + NH * TC * 2
    f_t1 = aalloc(af, "f_t1", [128, TC], F32)
    ffn_end = af[0]

    aa = [MS]
    a_rkpad = aalloc(aa, "a_rkpad", [128, NDM * (TC + 1)], BF16)
    a_vtm = aalloc(aa, "a_vtm", [128, 4 * 1024], BF16)
    a_wktm = aalloc(aa, "a_wktm", [128, 4 * 1024], BF16)
    a_SmT = aalloc(aa, "a_SmT", [128, 4 * TC], BF16)
    a_reads = aalloc(aa, "a_reads", [128, NDM * TC], BF16)
    aw_f32 = [aalloc(aa, f"aw_f32{i}", [128, 1024], F32) for i in range(4)]
    aw_bf = aalloc(aa, "aw_bf", [128, 1024], BF16)
    a_t1 = aalloc(aa, "a_t1", [128, TC], F32)
    attn_end = aa[0]

    arena_end = max(ap_[0], ffn_end, attn_end)
    if os.environ.get("ARENA_DEBUG"):
        print(f"ARENA: bytes={ARENA_BYTES} MS={MS} ap={ap_[0]} ffn={ffn_end} attn={attn_end} end={arena_end} slack={ARENA_BYTES-arena_end}")
    assert arena_end <= ARENA_BYTES, f"SBUF over: {arena_end} > {ARENA_BYTES}"

    pb = [nc.alloc_psum_tensor(f"pb{i}", [128, 512], F32) for i in range(6)]
    pb_dw = nc.alloc_psum_tensor("pb_dw", [128, 512], F32)
    pbT = nc.alloc_psum_tensor("pbT", [128, 128], BF16)

    sems = {}
    for s in ["SY", "PE", "DV", "AC", "PL"] + DMA_SEMS:
        sems[s] = es.enter_context(nc.semaphore(s))
    pl = Planner()
    G = Group()
    G2 = Group()
    G3 = Group()
    BUFS = {}

    def b(h):
        k = id(h)
        if k not in BUFS:
            BUFS[k] = Buf()
        return BUFS[k]

    def gb(h, tag, grp=None):
        bb = b(h)
        gr = G if grp is None else grp
        found = False
        for idx, (g0, _t) in enumerate(bb.groups):
            if g0 is gr:
                bb.groups[idx] = (gr, tag)
                found = True
        if not found:
            bb.groups.append((gr, tag))
        return bb

    scr_ready = Buf()
    st_bufs = [Buf() for _ in range(NDT * DS)]
    sp_bufs = {nm: [Buf() for _ in range(NDT)] for nm in ("sdt", "su", "sz", "sy")}
    wt_bufs = [Buf() for _ in range(8)]
    wtb_bufs = [Buf() for _ in range(8)]

    def sy_dma(dst, src, reads=(), writes=(), queue="SYQ"):
        return pl.op("sync", lambda e, d=dst, s=src: e.dma_start(out=d, in_=s),
                     reads=reads, writes=writes, dma_queue=queue)

    def gq_dma(dst, src, reads=(), writes=(), queue="GQQ"):
        return pl.op("gpsimd", lambda e, d=dst, s=src: e.dma_start(out=d, in_=s),
                     reads=reads, writes=writes, dma_queue=queue)

    def sc_dma(dst, src, reads=(), writes=(), queue="GQQ"):
        return pl.op("scalar", lambda e, d=dst, s=src: e.dma_start(out=d, in_=s),
                     reads=reads, writes=writes, dma_queue=queue)

    def dve(emit, reads=(), writes=()):
        return pl.op("vector", emit, reads=reads, writes=writes)

    def act(emit, reads=(), writes=()):
        return pl.op("scalar", emit, reads=reads, writes=writes)

    def pool(emit, reads=(), writes=()):
        return pl.op("gpsimd", emit, reads=reads, writes=writes)

    def pe(emit, reads=(), writes=()):
        return pl.op("tensor", emit, reads=reads, writes=writes)

    def mm_group(psum_ap, psum_buf, pairs):
        n = len(pairs)
        for i, (lt, rh, rbs) in enumerate(pairs):
            pl.op("tensor",
                  lambda e, o=psum_ap, l=lt, r=rh, st=(i == 0), sp=(i == n - 1):
                  e.matmul(o, l, r, start=st, stop=sp),
                  reads=list(rbs), writes=[psum_buf], defer=(i < n - 1))

    for dst, src, tagged in [
            (c_convw, convw_in, 0), (c_convb, convb_in, 0), (c_D, D_in, 0),
            (c_bdt, bdt_in, 0), (c_negA, Alog_in, 0), (c_scales, scales_in, 0),
            (c_pos, pos_in, 1), (c_revpos, revpos_in, 0),
            (s_decay, decay_in, 0), (s_logalpha, logalpha_in, 0),
            (p_diffs, diffs_in, 1), (p_causal, causal_in, 1),
            (p_wf1, wxdt_in, 1), (p_wf2, wxb_in, 1), (p_wf3, wxc_in, 1),
            (p_wf4, wdtT_in, 1)]:
        sy_dma(dst[:], src, writes=[gb(dst, "pro") if tagged else b(dst)], queue="SYC")
    sy_dma(ring_f[0][:, 0:128], ident_in, writes=[gb(ring_f[0], "pro")], queue="RF0")
    dve(lambda e: e.tensor_copy(c_ident[:], ring_f[0][:, 0:128]),
        reads=[gb(ring_f[0], "pro")], writes=[b(c_ident)])
    sy_dma(ring_f[0][0:16, 0:2048], sel_in, writes=[gb(ring_f[0], "pro")], queue="RF0")
    dve(lambda e: e.tensor_copy(c_sel[:], ring_f[0][0:16, 0:2048]),
        reads=[gb(ring_f[0], "pro")], writes=[b(c_sel)])
    dve(lambda e: e.memset(c_ones_r[:], 1.0), writes=[b(c_ones_r)])
    dve(lambda e: e.memset(c_ones_c[:], 1.0), writes=[b(c_ones_c)])
    dve(lambda e: e.memset(k_eps[:], 1e-5), writes=[b(k_eps)])
    dve(lambda e: e.memset(k_half[:], 0.5), writes=[b(k_half)])
    dve(lambda e: e.memset(k_neghalf[:], -0.5), writes=[b(k_neghalf)])
    dve(lambda e: e.memset(k_invD[:], 1.0 / D), writes=[b(k_invD)])
    dve(lambda e: e.memset(k_tc[:], float(TC)), writes=[b(k_tc)])
    dve(lambda e: e.tensor_copy(wxdt_b[:], p_wf1[:]), reads=[gb(p_wf1, "pro")], writes=[b(wxdt_b)])
    dve(lambda e: e.tensor_copy(wxb_b[:], p_wf2[:]), reads=[gb(p_wf2, "pro")], writes=[b(wxb_b)])
    dve(lambda e: e.tensor_copy(wxc_b[:], p_wf3[:]), reads=[gb(p_wf3, "pro")], writes=[b(wxc_b)])
    dve(lambda e: e.tensor_copy(wdt_b[:], p_wf4[:]), reads=[gb(p_wf4, "pro")], writes=[b(wdt_b)])
    act(lambda e: e.activation(c_negA[:], c_negA[:], AF.Exp), reads=[b(c_negA)], writes=[b(c_negA)])
    dve(lambda e: e.tensor_scalar_mul(c_negA[:], c_negA[:], -1.0), reads=[b(c_negA)], writes=[b(c_negA)])
    act(lambda e: e.activation(s_t1[:], s_decay[:], AF.Sigmoid), reads=[b(s_decay)], writes=[b(s_t1)])
    act(lambda e: e.activation(s_t1[:], s_t1[:], AF.Ln), reads=[b(s_t1)], writes=[b(s_t1)])
    pe(lambda e: e.matmul(pb[4][:, 0:1], c_ones_r[:], s_t1[:], start=True, stop=True),
       reads=[b(c_ones_r), b(s_t1)], writes=[b(pb[4])])
    act(lambda e: e.copy(k_logg[:], pb[4][:, 0:1]), reads=[b(pb[4])], writes=[b(k_logg)])
    act(lambda e: e.activation(k_gC[:], k_logg[:], AF.Exp, scale=k_tc[:, 0:1]),
        reads=[b(k_logg), b(k_tc)], writes=[b(k_gC)])
    act(lambda e: e.activation(s_t1[:], s_logalpha[:], AF.Exp),
        reads=[b(s_logalpha)], writes=[b(s_t1)])
    pe(lambda e: e.matmul(pb[4][:, 0:1], c_ones_r[:], s_t1[:], start=True, stop=True),
       reads=[b(c_ones_r), b(s_t1)], writes=[b(pb[4])])
    act(lambda e: e.copy(k_alpha[:], pb[4][:, 0:1]), reads=[b(pb[4])], writes=[b(k_alpha)])
    for kk in range(4):
        act(lambda e, kk=kk: e.activation(k_M[:, kk * TC:(kk + 1) * TC],
                                          p_diffs[:, kk * TC:(kk + 1) * TC],
                                          AF.Exp, scale=k_logg[:, 0:1]),
            reads=[gb(p_diffs, "pro"), b(k_logg)], writes=[b(k_M)])
    dve(lambda e: e.tensor_mul(k_M[:], k_M[:], p_causal[:]),
        reads=[b(k_M), gb(p_causal, "pro")], writes=[b(k_M)])
    act(lambda e: e.activation(s_row[:], c_pos[:], AF.Exp, scale=k_logg[0:1, 0:1]),
        reads=[gb(c_pos, "pro"), b(k_logg)], writes=[gb(s_row, "pro")])
    pe(lambda e: e.matmul(pb[4][:, :], c_ones_r[:], s_row[:], start=True, stop=True),
       reads=[b(c_ones_r), gb(s_row, "pro")], writes=[b(pb[4])])
    act(lambda e: e.copy(k_gp[:], pb[4][:, :]), reads=[b(pb[4])], writes=[b(k_gp)])
    act(lambda e: e.activation(k_gw[:], c_revpos[:], AF.Exp, scale=k_logg[:, 0:1]),
        reads=[b(c_revpos), b(k_logg)], writes=[b(k_gw)])
    dve(lambda e: e.memset(ring_f[0][:, :], 0.0), writes=[gb(ring_f[0], "pro")])
    dve(lambda e: e.memset(ring[0][:, 0:1024], 0.0), writes=[b(ring[0])])
    for jt in range(8):
        gq_dma(scr_WT[jt], ring_f[0][:, 0:1024], reads=[gb(ring_f[0], "pro")],
               writes=[wt_bufs[jt]], queue="WTS")
        gq_dma(scr_WTb[jt], ring[0][:, 0:1024], reads=[b(ring[0])], writes=[wtb_bufs[jt]],
               queue="WTS")
    tok_st = dve(lambda e: e.memset(st_scan[:], 0.0), writes=[b(st_scan)])
    for sb_ in st_bufs:
        sb_.ready = {tok_st[0]: tok_st[1]}
    dve(lambda e: e.memset(st_ctail[:], 0.0), writes=[b(st_ctail)])
    dve(lambda e: e.memset(st_wktail[:], 0.0), writes=[b(st_wktail)])

    fam_ready = {}
    for a in range(WSPECS["w_inT"][0]):
        gq_dma(w_scr["w_inT"][a], w_f32["w_inT"][a], queue="CW%d" % (a % 2))
    fr_in = Buf()
    fr_in.ready = {"CW0": pl.count["CW0"], "CW1": pl.count["CW1"]}
    fam_ready["w_inT"] = fr_in
    rest_qs = ("CW2", "CW3", "CW4")
    ci = 0
    for nm in ["w_outT", "w1T", "w2T", "h1T", "h2T", "w3T", "w_writeT", "w_readT"]:
        ns, kt, ew = WSPECS[nm]
        for a in range(ns):
            gq_dma(w_scr[nm][a], w_f32[nm][a], queue=rest_qs[ci % 3])
            ci += 1
    fr_all = Buf()
    fr_all.ready = {q: pl.count[q] for q in rest_qs}
    for nm in ["w_outT", "w1T", "w2T", "h1T", "h2T", "w3T", "w_writeT", "w_readT"]:
        fam_ready[nm] = fr_all

    gq_dma(x0_next[:], x0_in[0].rearrange("p j t -> p (j t)"), writes=[b(x0_next)], queue="X0Q")

    rstate = [0]
    def load_wslice(nm, s, ring_set=None):
        ns, kt, ew = WSPECS[nm]
        if ring_set is None:
            si = rstate[0] % 4
        else:
            si = ring_set[rstate[0] % len(ring_set)]
        rstate[0] += 1
        slot = ring[si]
        n = kt * ew
        sy_dma(slot[:, 0:n], w_scr[nm][s].rearrange("p k e -> p (k e)"),
               reads=[fam_ready[nm]], writes=[b(slot)], queue="RS%d" % si)
        return slot

    def layernorm(xln, t1buf, tag, dest, fold=False, src_t=None):
        if src_t is None:
            src_t = x_res
        xl = gb(xln, tag) if tag else b(xln)
        t1b = gb(t1buf, tag) if tag else b(t1buf)
        for j in range(NDM):
            sl = slice(j * TC, (j + 1) * TC)
            lb = lt_bf[0]
            dve(lambda e, sl=sl, lb=lb: e.tensor_copy(lb[:], src_t[:, sl]),
                reads=[b(src_t)], writes=[b(lb)])
            pe(lambda e, lb=lb, st=(j == 0), sp=(j == NDM - 1):
               e.matmul(pb[4][0:1, :], c_ones_c[:], lb[:], start=st, stop=sp),
               reads=[b(lb), b(c_ones_c)], writes=[b(pb[4])])
        for j in range(NDM):
            sl = slice(j * TC, (j + 1) * TC)
            lb = lt_bf[0]
            act(lambda e, sl=sl, lb=lb: e.activation(lb[:], src_t[:, sl], AF.Square),
                reads=[b(src_t)], writes=[b(lb)])
            pe(lambda e, lb=lb, st=(j == 0), sp=(j == NDM - 1):
               e.matmul(pb[5][0:1, :], c_ones_c[:], lb[:], start=st, stop=sp),
               reads=[b(lb), b(c_ones_c)], writes=[b(pb[5])])
        act(lambda e: e.activation(l_m[:], pb[4][0:1, :], AF.Identity, scale=k_invD[0:1, 0:1]),
            reads=[b(pb[4]), b(k_invD)], writes=[b(l_m)])
        act(lambda e: e.activation(l_v[:], pb[5][0:1, :], AF.Identity, scale=k_invD[0:1, 0:1]),
            reads=[b(pb[5]), b(k_invD)], writes=[b(l_v)])
        act(lambda e: e.activation(l_mr[:], l_m[:], AF.Square), reads=[b(l_m)], writes=[b(l_mr)])
        dve(lambda e: e.tensor_sub(l_v[:], l_v[:], l_mr[:]), reads=[b(l_v), b(l_mr)], writes=[b(l_v)])
        act(lambda e: e.activation(l_r[:], l_v[:], AF.Ln, bias=k_eps[0:1, 0:1]),
            reads=[b(l_v), b(k_eps)], writes=[b(l_r)])
        act(lambda e: e.activation(l_r[:], l_r[:], AF.Exp, scale=k_neghalf[0:1, 0:1]),
            reads=[b(l_r), b(k_neghalf)], writes=[b(l_r)])
        dve(lambda e: e.tensor_mul(l_mr[:], l_m[:], l_r[:]), reads=[b(l_m), b(l_r)], writes=[b(l_mr)])
        pe(lambda e: e.matmul(pb[4][:, :], c_ones_r[:], l_r[:], start=True, stop=True),
           reads=[b(l_r), b(c_ones_r)], writes=[b(pb[4])])
        act(lambda e: e.copy(l_rrep[:], pb[4][:, :]), reads=[b(pb[4])], writes=[b(l_rrep)])
        pe(lambda e: e.matmul(pb[5][:, :], c_ones_r[:], l_mr[:], start=True, stop=True),
           reads=[b(l_mr), b(c_ones_r)], writes=[b(pb[5])])
        act(lambda e: e.copy(l_mrrep[:], pb[5][:, :]), reads=[b(pb[5])], writes=[b(l_mrrep)])
        for j in range(NDM):
            sl = slice(j * TC, (j + 1) * TC)
            dve(lambda e, sl=sl: e.tensor_mul(t1buf[:], src_t[:, sl], l_rrep[:]),
                reads=[b(src_t), b(l_rrep)], writes=[t1b])
            dve(lambda e, d=dest(j): e.tensor_sub(d, t1buf[:], l_mrrep[:]),
                reads=[t1b, b(l_mrrep)], writes=[xl])
            if fold:
                dve(lambda e, sl=sl: e.tensor_add(x_res[:, sl], x_res[:, sl], t1buf[:]),
                    reads=[t1b], writes=[b(x_res)])
                dve(lambda e, sl=sl: e.tensor_sub(x_res[:, sl], x_res[:, sl], l_mrrep[:]),
                    reads=[b(l_mrrep)], writes=[b(x_res)])

    def ffn(tag, adv):
        gb(f_xln, "x" + tag, G3)
        layernorm(f_xln, f_t1, tag, lambda j: f_xln[:, j * TC:(j + 1) * TC])
        fx, fh = gb(f_xln, tag), gb(f_h, tag)
        fx = gb(f_xln, "x" + tag, G3)
        fg = gb(f_g, tag)
        fg = gb(f_g, "g" + tag, G3)
        ft1 = gb(f_t1, tag)
        for s in range(8):
            slot1 = load_wslice("w1T", s)
            slot2 = load_wslice("w2T", s)
            for m in range(4):
                et = 4 * s + m
                p1, p2 = pb[(2 * m) % 6], pb[(2 * m + 1) % 6]
                mm_group(p1[:, :], b(p1),
                         [(slot1[:, k * 512 + m * 128: k * 512 + (m + 1) * 128],
                           f_xln[:, k * TC:(k + 1) * TC], [b(slot1), fx]) for k in range(NDM)])
                mm_group(p2[:, :], b(p2),
                         [(slot2[:, k * 512 + m * 128: k * 512 + (m + 1) * 128],
                           f_xln[:, k * TC:(k + 1) * TC], [b(slot2), fx]) for k in range(NDM)])
                act(lambda e, p=p1, et=et: e.activation(f_t1[:], p[:, :], AF.Silu,
                                                        scale=c_scales[:, et:et + 1]),
                    reads=[b(p1), b(c_scales)], writes=[ft1])
                dve(lambda e, p=p2, et=et: e.scalar_tensor_tensor(
                        f_h[:, et * TC:(et + 1) * TC], p[:, :],
                        c_scales[:, NH + et:NH + et + 1], f_t1[:], ALU.mult, ALU.mult),
                    reads=[b(p2), b(c_scales), ft1], writes=[fh])
            adv(1.0)
        for wnm, src, srcb, dst, dstb, so in (("h1T", f_h, fh, f_g, fg, 2 * NH),
                                              ("h2T", f_g, fg, f_h, fh, 3 * NH)):
            def gelu_evac(et, p):
                act(lambda e, p=p, et=et, so=so, dst=dst: e.activation(
                        dst[:, et * TC:(et + 1) * TC], p[:, :], AF.Gelu,
                        scale=c_scales[:, so + et:so + et + 1]),
                    reads=[b(p), b(c_scales)], writes=[dstb])
            for s in range(32):
                slot = load_wslice(wnm, s)
                p = pb[s % 6]
                mm_group(p[:, :], b(p),
                         [(slot[:, k * 128: (k + 1) * 128],
                           src[:, k * TC:(k + 1) * TC], [b(slot), srcb]) for k in range(NH)])
                if s % 2 == 1:
                    gelu_evac(s - 1, pb[(s - 1) % 6])
                    gelu_evac(s, p)
                    adv(1.5)
        for s in range(8):
            slot = load_wslice("w3T", s)
            et = s
            p = pb[s % 4]
            mm_group(p[:, :], b(p),
                     [(slot[:, k * 128: (k + 1) * 128],
                       f_h[:, k * TC:(k + 1) * TC], [b(slot), fh]) for k in range(NH)])
            dve(lambda e, p=p, et=et: e.scalar_tensor_tensor(
                    x_res[:, et * TC:(et + 1) * TC], p[:, :],
                    c_scales[:, 4 * NH + et:4 * NH + et + 1],
                    x_res[:, et * TC:(et + 1) * TC], ALU.mult, ALU.add),
                reads=[b(p), b(c_scales), b(x_res)], writes=[b(x_res)])
            adv(0.5)

    def mamba_pre(c):
        mt1 = b(m_t1)
        mpad = b(mb_pad)
        gb(m_xln, f"x{c}", G2)
        layernorm(m_xln, m_t1, None, lambda j: m_xln[:, j * TC:(j + 1) * TC], src_t=x0_next)
        mx = b(m_xln)
        for s in range(8):
            slot = load_wslice("w_inT", s)
            for m in range(4):
                et = 4 * s + m
                p = pb[et % 4]
                mm_group(p[:, :], b(p),
                         [(slot[:, k * 512 + m * 128: k * 512 + (m + 1) * 128],
                           m_xln[:, k * TC:(k + 1) * TC], [b(slot), mx]) for k in range(NDM)])
                if et < NDT:
                    i = et
                    dve(lambda e, i=i: e.tensor_copy(mb_pad[:, 0:3], st_ctail[:, i * 3: i * 3 + 3]),
                        reads=[b(st_ctail)], writes=[mpad])
                    dve(lambda e, p=p: e.tensor_scalar(mb_pad[:, 3:3 + TC], p[:, :],
                                                       0.0, 1.0, ALU.max, ALU.min),
                        reads=[b(p)], writes=[mpad])
                    dve(lambda e, i=i: e.tensor_copy(st_ctail[:, i * 3: i * 3 + 3],
                                                     mb_pad[:, TC: TC + 3]),
                        reads=[mpad], writes=[b(st_ctail)])
                    dve(lambda e, i=i: e.tensor_scalar(
                            m_t1[:], mb_pad[:, 0:TC], c_convw[:, 4 * i:4 * i + 1], None, ALU.mult),
                        reads=[mpad, b(c_convw)], writes=[mt1])
                    for q in (1, 2, 3):
                        dve(lambda e, i=i, q=q: e.scalar_tensor_tensor(
                                m_t1[:], mb_pad[:, q:q + TC],
                                c_convw[:, 4 * i + q:4 * i + q + 1], m_t1[:], ALU.mult, ALU.add),
                            reads=[mpad, b(c_convw), mt1], writes=[mt1])
                    dve(lambda e, i=i: e.tensor_scalar(
                            m_t1[:], m_t1[:], c_convb[:, i:i + 1], None, ALU.add),
                        reads=[b(c_convb), mt1], writes=[mt1])
                    act(lambda e: e.activation(lt_bf[0][:], m_t1[:], AF.Tanh, scale=k_half[:, 0:1]),
                        reads=[mt1, b(k_half)], writes=[b(lt_bf[0])])
                    dve(lambda e: e.scalar_tensor_tensor(m_t1[:], lt_bf[0][:], 1.0, m_t1[:],
                                                         ALU.add, ALU.mult),
                        reads=[mt1, b(lt_bf[0])], writes=[mt1])
                    dve(lambda e: e.tensor_scalar(m_t1[:], m_t1[:], 0.5, 0.0, ALU.mult, ALU.max),
                        reads=[mt1], writes=[mt1])
                    ev = mb_ev[i % 3]
                    dve(lambda e, ev=ev: e.tensor_scalar(ev[:], m_t1[:], 1.0, None, ALU.min),
                        reads=[mt1], writes=[b(ev)])
                    gq_dma(scr_sp["su"][i], ev[:], reads=[b(ev)], writes=[sp_bufs["su"][i]],
                           queue="SPU")
                    if YIELD_XZ:
                        yield
                else:
                    i = et - NDT
                    ev = mb_ev[i % 3]
                    dve(lambda e, p=p, ev=ev: e.tensor_scalar(ev[:], p[:, :],
                                                              0.0, 1.0, ALU.max, ALU.min),
                        reads=[b(p)], writes=[b(ev)])
                    gq_dma(scr_sp["sz"][i], ev[:], reads=[b(ev)], writes=[sp_bufs["sz"][i]],
                           queue="SPZ")
                    if YIELD_XZ:
                        yield
        uslots = []
        for g2_ in range(2):
            slot = ring[g2_]
            sy_dma(slot[:, :].rearrange("p (a t) -> p a t", a=8),
                   scr_sp["su"][8 * g2_:8 * g2_ + 8].rearrange("a p t -> p a t"),
                   reads=[sp_bufs["su"][8 * g2_ + q2] for q2 in range(8)], writes=[b(slot)],
                   queue="RS%d" % g2_)
            uslots.append(slot)
        def uview(k):
            return uslots[k // 8][:, (k % 8) * TC:(k % 8 + 1) * TC]
        mm_group(pb[0][0:64, :], b(pb[0]),
                 [(wxdt_b[:, k * DTR:(k + 1) * DTR], uview(k),
                   [b(wxdt_b), b(uslots[k // 8])]) for k in range(NDT)])
        dve(lambda e: e.tensor_scalar(m_dtraw[:], pb[0][0:64, :], 0.0, 1.0, ALU.max, ALU.min),
            reads=[b(pb[0])], writes=[b(m_dtraw)])
        mm_group(pb[1][0:16, :], b(pb[1]),
                 [(wxb_b[:, k * 16:(k + 1) * 16], uview(k),
                   [b(wxb_b), b(uslots[k // 8])]) for k in range(NDT)])
        dve(lambda e: e.tensor_copy(m_b[:], pb[1][0:16, :]), reads=[b(pb[1])], writes=[b(m_b)])
        mm_group(pb[2][0:16, :], b(pb[2]),
                 [(wxc_b[:, k * 16:(k + 1) * 16], uview(k),
                   [b(wxc_b), b(uslots[k // 8])]) for k in range(NDT)])
        dve(lambda e: e.tensor_copy(m_c[:], pb[2][0:16, :]), reads=[b(pb[2])], writes=[b(m_c)])
        if YIELD_PRE:
            yield
        for i in range(NDT):
            if YIELD_PRE and i % 2 == 0:
                yield
            p = pb[2 + i % 2]
            pe(lambda e, i=i, p=p: e.matmul(p[:, :], wdt_b[:, i * 128:(i + 1) * 128],
                                            m_dtraw[:], start=True, stop=True),
               reads=[b(wdt_b), b(m_dtraw)], writes=[b(p)])
            act(lambda e, i=i, p=p: e.activation(m_t1[:], p[:, :], AF.Exp,
                                                 bias=c_bdt[:, i:i + 1]),
                reads=[b(p), b(c_bdt)], writes=[mt1])
            act(lambda e: e.activation(m_t1[:], m_t1[:], AF.Ln, bias=1.0),
                reads=[mt1], writes=[mt1])
            ev = mb_ev[i % 3]
            dve(lambda e, ev=ev: e.tensor_scalar(ev[:], m_t1[:], 1e-4, 20.0, ALU.max, ALU.min),
                reads=[mt1], writes=[b(ev)])
            gq_dma(scr_sp["sdt"][i], ev[:], reads=[b(ev)], writes=[sp_bufs["sdt"][i]],
                   queue="SPDT")

    def scan_gen(c):
        mball = gb(m_Ball, f"bc{c}", G2)
        mcall, myacc, mytmp = b(m_Call), b(m_yacc), b(m_ytmp)
        mt2 = b(m_t2)
        def prefetch_scan(i, off, ri):
            for nm, bufs, qs in (("sdt", ms_dt, "MSD"), ("su", ms_u, "MSU"), ("sz", ms_z, "MSZ")):
                sc_dma(bufs[ri][:], scr_sp[nm][i][:, off:off + HC],
                       reads=[sp_bufs[nm][i]], writes=[b(bufs[ri])], queue=qs + str(ri))
        for hf in range(2):
            off = hf * HC
            for n in range(DS):
                pe(lambda e, n=n, off=off: e.matmul(pb[4][:, 0:HC], c_sel[:, n * 128:(n + 1) * 128],
                                           m_b[:, off:off + HC], start=True, stop=True),
                   reads=[b(c_sel), b(m_b)], writes=[b(pb[4])])
                act(lambda e, n=n: e.activation(m_Ball[:, n * HC:(n + 1) * HC],
                                                pb[4][:, 0:HC], AF.Identity),
                    reads=[b(pb[4])], writes=[mball])
                pe(lambda e, n=n, off=off: e.matmul(pb[5][:, 0:HC], c_sel[:, n * 128:(n + 1) * 128],
                                           m_c[:, off:off + HC], start=True, stop=True),
                   reads=[b(c_sel), b(m_c)], writes=[b(pb[5])])
                act(lambda e, n=n: e.activation(m_Call[:, n * HC:(n + 1) * HC],
                                                pb[5][:, 0:HC], AF.Identity),
                    reads=[b(pb[5])], writes=[mcall])
            if hf == 0 and GEN_LEVEL >= 1:
                prefetch_scan(0, 0, 0)
            for i in range(NDT):
                ri = i % 2
                sdt_b, su_b, sz_b = ms_dt[ri], ms_u[ri], ms_z[ri]
                if GEN_LEVEL >= 1:
                    if i + 1 < NDT:
                        prefetch_scan(i + 1, off, (i + 1) % 2)
                    elif hf == 0:
                        prefetch_scan(0, HC, 0)
                if GEN_LEVEL <= 1:
                    yield
                    continue
                dtu = m_dtu2[i % 2]
                dve(lambda e, dtu=dtu, sdt_b=sdt_b, su_b=su_b: e.tensor_mul(
                        dtu[:], sdt_b[:], su_b[:]),
                    reads=[b(sdt_b), b(su_b)], writes=[b(dtu)])
                def emit_dA(n):
                    g = i * DS + n
                    dA = m_dA[g % 8]
                    act(lambda e, g=g, dA=dA, sdt_b=sdt_b: e.activation(
                            dA[:], sdt_b[:], AF.Exp, scale=c_negA[:, g:g + 1]),
                        reads=[b(sdt_b), b(c_negA)], writes=[b(dA)])
                def emit_dBu(n):
                    g = i * DS + n
                    dBu = m_dBu[g % 3]
                    dve(lambda e, n=n, dBu=dBu, dtu=dtu: e.tensor_mul(
                            dBu[:], dtu[:], m_Ball[:, n * HC:(n + 1) * HC]),
                        reads=[b(dtu), mball], writes=[b(dBu)])
                emit_dBu(0)
                for n in range(DS):
                    g = i * DS + n
                    dA = m_dA[g % 8]
                    dBu = m_dBu[g % 3]
                    hh = m_h[g % 3]
                    if n % 8 == 0:
                        for q in range(8):
                            emit_dA(n + q)
                    if SCAN_AS_MUL:
                        dve(lambda e, g=g, dA=dA, dBu=dBu, hh=hh: e.tensor_mul(
                                hh[:], dA[:], dBu[:]),
                            reads=[b(dA), b(dBu), st_bufs[g]], writes=[b(hh)])
                    else:
                        dve(lambda e, g=g, dA=dA, dBu=dBu, hh=hh: e.tensor_tensor_scan(
                                hh[:], dA[:], dBu[:], st_scan[:, g:g + 1], ALU.mult, ALU.add),
                            reads=[b(dA), b(dBu), st_bufs[g]], writes=[b(hh)])
                    if n + 1 < DS:
                        emit_dBu(n + 1)
                    pool(lambda e, g=g, hh=hh: e.tensor_copy(st_scan[:, g:g + 1],
                                                             hh[:, HC - 1:HC]),
                         reads=[b(hh)], writes=[st_bufs[g]])
                    if n == 0:
                        dve(lambda e, n=n, hh=hh: e.tensor_mul(
                                m_yacc[:], hh[:], m_Call[:, n * HC:(n + 1) * HC]),
                            reads=[b(hh), mcall], writes=[myacc])
                    else:
                        dve(lambda e, n=n, hh=hh: e.tensor_mul(
                                m_ytmp[:], hh[:], m_Call[:, n * HC:(n + 1) * HC]),
                            reads=[b(hh), mcall], writes=[mytmp])
                        dve(lambda e: e.tensor_add(m_yacc[:], m_yacc[:], m_ytmp[:]),
                            reads=[mytmp], writes=[myacc])
                    if n % 8 == 7 and n + 1 < DS:
                        yield
                dve(lambda e, su_b=su_b, i=i: e.scalar_tensor_tensor(
                        m_yacc[:], su_b[:], c_D[:, i:i + 1], m_yacc[:], ALU.mult, ALU.add),
                    reads=[b(su_b), b(c_D)], writes=[myacc])
                dve(lambda e: e.tensor_scalar(m_ytmp[:], m_yacc[:], 0.0, 1.0, ALU.max, ALU.min),
                    reads=[myacc], writes=[mytmp])
                act(lambda e, sz_b=sz_b: e.activation(m_t2[:, 0:HC], sz_b[:],
                                                      AF.Tanh, scale=k_half[:, 0:1]),
                    reads=[b(sz_b), b(k_half)], writes=[mt2])
                dve(lambda e, sz_b=sz_b: e.scalar_tensor_tensor(
                        m_t2[:, 0:HC], m_t2[:, 0:HC], 1.0, sz_b[:], ALU.add, ALU.mult),
                    reads=[mt2, b(sz_b)], writes=[mt2])
                dve(lambda e: e.tensor_mul(m_ytmp[:], m_ytmp[:], m_t2[:, 0:HC]),
                    reads=[mt2], writes=[mytmp])
                yb = ms_y[i % 2]
                dve(lambda e, yb=yb: e.tensor_scalar(yb[:], m_ytmp[:],
                                                     0.5, 1.0, ALU.mult, ALU.min),
                    reads=[mytmp], writes=[b(yb)])
                gq_dma(scr_sp["sy"][i][:, off:off + HC], yb[:], reads=[b(yb)],
                       writes=[sp_bufs["sy"][i]], queue="SPY")
                yield

    def wout(c):
        yslots = []
        for g2_ in range(2):
            slot = ring[g2_]
            sy_dma(slot[:, :].rearrange("p (a t) -> p a t", a=8),
                   scr_sp["sy"][8 * g2_:8 * g2_ + 8].rearrange("a p t -> p a t"),
                   reads=[sp_bufs["sy"][8 * g2_ + q2] for q2 in range(8)], writes=[b(slot)],
                   queue="RS%d" % g2_)
            yslots.append(slot)
        def yview(k):
            return yslots[k // 8][:, (k % 8) * TC:(k % 8 + 1) * TC]
        for s in range(4):
            slot = load_wslice("w_outT", s, ring_set=(2, 3))
            for m in range(2):
                et = 2 * s + m
                p = pb[et % 4]
                mm_group(p[:, :], b(p),
                         [(slot[:, k * 256 + m * 128: k * 256 + (m + 1) * 128],
                           yview(k), [b(slot), b(yslots[k // 8])]) for k in range(NDT)])
                dve(lambda e, p=p, et=et: e.tensor_add(x_res[:, et * TC:(et + 1) * TC],
                                                       p[:, :], x0_next[:, et * TC:(et + 1) * TC]),
                    reads=[b(p), b(x0_next)], writes=[b(x_res)])
        if c + 1 < nchunk:
            gq_dma(x0_next[:], x0_in[c + 1].rearrange("p j t -> p (j t)"), writes=[b(x0_next)],
                   queue="X0Q")

    def attnb_fn(c, adv):
        at = f"a{c}"
        rkp = gb(a_rkpad, at)
        for j in range(NDM):
            dve(lambda e, j=j: e.tensor_copy(a_rkpad[:, j * 513: j * 513 + 1],
                                             st_wktail[:, j:j + 1]),
                reads=[b(st_wktail)], writes=[rkp])
        layernorm(a_rkpad, a_t1, at, lambda j: a_rkpad[:, j * 513 + 1: j * 513 + 513], fold=True)
        for j in range(NDM):
            dve(lambda e, j=j: e.tensor_copy(st_wktail[:, j:j + 1],
                                             a_rkpad[:, j * 513 + 512: j * 513 + 513]),
                reads=[rkp], writes=[b(st_wktail)])
        adv()
        for eh in range(2):
            slot = load_wslice("w_writeT", eh, ring_set=(2, 3))
            for ct in range(4):
                p = pb[ct % 2]
                mm_group(p[:, :], b(p),
                         [(a_rkpad[:, k * 513 + 1 + ct * 128: k * 513 + 1 + (ct + 1) * 128],
                           slot[:, k * 512:(k + 1) * 512], [rkp, b(slot)]) for k in range(NDM)])
                dve(lambda e, p=p, ct=ct, eh=eh: e.tensor_copy(
                        a_vtm[:, ct * 1024 + eh * 512: ct * 1024 + (eh + 1) * 512], p[:, :]),
                    reads=[b(p)], writes=[gb(a_vtm, at)])
            adv()
        for kt in range(4):
            p = pb[2 + kt % 2]
            mm_group(p[:, :], b(p),
                     [(a_rkpad[:, k * 513 + kt * 128: k * 513 + (kt + 1) * 128],
                       a_rkpad[:, k * 513 + 1: k * 513 + 513], [rkp]) for k in range(NDM)])
            dve(lambda e, p=p, kt=kt: e.tensor_mul(a_SmT[:, kt * TC:(kt + 1) * TC],
                                                   p[:, :], k_M[:, kt * TC:(kt + 1) * TC]),
                reads=[b(p), b(k_M)], writes=[gb(a_SmT, at)])
        adv()
        for ct in range(4):
            for jd in range(NDM):
                pe(lambda e, ct=ct, jd=jd: e.transpose(
                        pbT[:, :], a_rkpad[:, jd * 513 + ct * 128: jd * 513 + (ct + 1) * 128],
                        c_ident[:]),
                   reads=[rkp, b(c_ident)], writes=[b(pbT)])
                dve(lambda e, ct=ct, jd=jd: e.tensor_scalar(
                        a_wktm[:, ct * 1024 + jd * 128: ct * 1024 + (jd + 1) * 128], pbT[:, :],
                        k_gw[:, ct:ct + 1], None, ALU.mult),
                    reads=[b(pbT), b(k_gw)], writes=[gb(a_wktm, at)])
            adv()
        for j in range(NDM):
            dve(lambda e, j=j: e.tensor_mul(a_rkpad[:, j * 513 + 1: j * 513 + 513],
                                            a_rkpad[:, j * 513 + 1: j * 513 + 513], k_gp[:]),
                reads=[b(k_gp), rkp], writes=[rkp])
        wtb_slots = []
        for g2_ in range(2):
            slot = ring[g2_]
            sy_dma(slot[:, :].rearrange("p (a t) -> p a t", a=4),
                   scr_WTb[4 * g2_:4 * g2_ + 4].rearrange("a p t -> p a t"),
                   reads=[wtb_bufs[4 * g2_ + q2] for q2 in range(4)], writes=[b(slot)],
                   queue="RS%d" % g2_)
            wtb_slots.append(slot)
        def wtbview(jt, i):
            return wtb_slots[jt // 4][:, (jt % 4) * 1024 + i * 128: (jt % 4) * 1024 + (i + 1) * 128]
        for i in range(NDM):
            p = pb[i % 2]
            pairs = [(wtbview(jt, i),
                      a_rkpad[:, jt * 513 + 1: jt * 513 + 513], [b(wtb_slots[jt // 4]), rkp])
                     for jt in range(8)]
            pairs += [(a_vtm[:, ct * 1024 + i * 128: ct * 1024 + (i + 1) * 128],
                       a_SmT[:, ct * TC:(ct + 1) * TC], [gb(a_vtm, at), gb(a_SmT, at)])
                      for ct in range(4)]
            mm_group(p[:, :], b(p), pairs)
            dve(lambda e, p=p, i=i: e.tensor_copy(a_reads[:, i * TC:(i + 1) * TC], p[:, :]),
                reads=[b(p)], writes=[gb(a_reads, at)])
            if i % 2 == 1:
                adv()
        for eh in range(2):
            slot = load_wslice("w_readT", eh, ring_set=(2, 3))
            for e2 in range(4):
                g2 = 4 * eh + e2
                p = pb[2 + e2 % 2]
                mm_group(p[:, :], b(p),
                         [(slot[:, k * 512 + e2 * 128: k * 512 + (e2 + 1) * 128],
                           a_reads[:, k * TC:(k + 1) * TC], [b(slot), gb(a_reads, at)])
                          for k in range(NDM)])
                dve(lambda e, p=p, g2=g2: e.scalar_tensor_tensor(
                        x_res[:, g2 * TC:(g2 + 1) * TC], p[:, :], k_alpha[:, 0:1],
                        x_res[:, g2 * TC:(g2 + 1) * TC], ALU.mult, ALU.add),
                    reads=[b(p), b(k_alpha), b(x_res)], writes=[b(x_res)])
            adv()
        for jt in range(4):
            awf = aw_f32[jt]
            sc_dma(awf[:], scr_WT[jt], reads=[wt_bufs[jt]], writes=[gb(awf, at)],
                   queue="AW%d" % (jt % 2))
        for jt in range(8):
            awf = aw_f32[jt % 4]
            awb = gb(awf, at)
            abb = gb(aw_bf, at)
            for ih in range(2):
                p = pb_dw if ih == 0 else pb[0]
                mm_group(p[:, :], b(p),
                         [(a_wktm[:, ct * 1024 + jt * 128: ct * 1024 + (jt + 1) * 128],
                           a_vtm[:, ct * 1024 + ih * 512: ct * 1024 + (ih + 1) * 512],
                           [gb(a_wktm, at), gb(a_vtm, at)]) for ct in range(4)])
                dve(lambda e, awf=awf, ih=ih, p=p: e.scalar_tensor_tensor(
                        awf[:, ih * 512:(ih + 1) * 512], awf[:, ih * 512:(ih + 1) * 512],
                        k_gC[:, 0:1], p[:, :], ALU.mult, ALU.add),
                    reads=[awb, b(k_gC), b(p)], writes=[awb])
            dve(lambda e, awf=awf: e.tensor_copy(aw_bf[:], awf[:]),
                reads=[awb], writes=[abb])
            gq_dma(scr_WT[jt], awf[:], reads=[awb], writes=[wt_bufs[jt]], queue="WTS")
            gq_dma(scr_WTb[jt], aw_bf[:], reads=[abb], writes=[wtb_bufs[jt]], queue="WTS")
            if jt + 4 < 8:
                awn = aw_f32[(jt + 4) % 4]
                sc_dma(awn[:], scr_WT[jt + 4], reads=[wt_bufs[jt + 4], gb(awn, at)],
                       writes=[gb(awn, at)], queue="AW%d" % (jt % 2))
            if jt % 2 == 1:
                adv()

    gen_holder = []
    adv_acc = [0.0]
    def adv(w=1.0):
        adv_acc[0] += w
        k = int(adv_acc[0])
        adv_acc[0] -= k
        for _ in range(k):
            if not gen_holder:
                break
            if next(gen_holder[0], "END") == "END":
                gen_holder.pop(0)
    def drain():
        while gen_holder:
            if next(gen_holder[0], "END") == "END":
                gen_holder.pop(0)

    for c in range(nchunk):
        if c == 0:
            gen_holder.append(mamba_pre(0))
            gen_holder.append(scan_gen(0))
            drain()
        wout(c)
        if dbg == 1:
            gq_dma(dbg_o["dbg_x1"][c], x_res[:], reads=[b(x_res)], queue="STQ")
        if c + 1 < nchunk:
            gen_holder.append(mamba_pre(c + 1))
            gen_holder.append(scan_gen(c + 1))
            if DRAIN_STAGE == 0:
                drain()
        ffn(f"f{c}a", adv)
        if DRAIN_STAGE == 1:
            drain()
        if dbg == 1:
            gq_dma(dbg_o["dbg_x2"][c], x_res[:], reads=[b(x_res)], queue="STQ")
        attnb_fn(c, adv)
        if DRAIN_STAGE == 2:
            drain()
        if dbg == 1:
            gq_dma(dbg_o["dbg_x3"][c], x_res[:], reads=[b(x_res)], queue="STQ")
        ffn(f"f{c}b", adv)
        drain()
        gq_dma(out_fm[c], x_res[:].rearrange("p (j t) -> p j t", j=NDM),
               reads=[b(x_res)], queue="STQ")

    if dbg == 2:
        for nm, scr in (("dbg_sdt", scr_sp["sdt"]), ("dbg_su", scr_sp["su"]),
                        ("dbg_sz", scr_sp["sz"]), ("dbg_sy", scr_sp["sy"])):
            gq_dma(dbg_o[nm].rearrange("a p t -> p a t"), scr.rearrange("a p t -> p a t"),
                   reads=[sp_bufs[nm[4:]][q2] for q2 in range(NDT)], queue="STQ")
        gq_dma(dbg_o["dbg_wtb"].rearrange("a p t -> p a t"), scr_WTb.rearrange("a p t -> p a t"),
               reads=wtb_bufs, queue="STQ")
        gq_dma(dbg_o["dbg_wtf"].rearrange("a p t -> p a t"), scr_WT.rearrange("a p t -> p a t"),
               reads=wt_bufs, queue="STQ")
    pl.op("gpsimd", lambda e: e.engine_nop(), extra=[("STQ", pl.count["STQ"])])

    with nc.Block() as block:
        def runner(name):
            own = ENG_SEMS[name]
            skip_self = name == "tensor"
            def run(eng):
                last = {}
                for waits, emit, inc in pl.ops[name]:
                    for s in sorted(waits):
                        v = waits[s]
                        if skip_self and s == own:
                            continue
                        if last.get(s, -1) < v:
                            eng.wait_ge(sems[s], v)
                            last[s] = v
                    inst = emit(eng)
                    if inc is not None:
                        inst.then_inc(sems[inc[0]], inc[1])
            return run
        block.sync(runner("sync"))
        block.tensor(runner("tensor"))
        block.vector(runner("vector"))
        block.scalar(runner("scalar"))
        block.gpsimd(runner("gpsimd"))

    es.close()
    return nc


def _tile_w(WT2, ew):
    K, M = WT2.shape
    kt, ns = K // 128, M // ew
    return np.ascontiguousarray(WT2.reshape(kt, 128, ns, ew).transpose(2, 1, 0, 3))


def prep_inputs(inputs, nchunk=NCHUNK):
    f32 = lambda x: np.ascontiguousarray(np.asarray(x, np.float32))
    W_in = f32(inputs["L0_attn_a_W_in"])
    conv_w = f32(inputs["L0_attn_a_conv_w"]).reshape(DI, 4)
    conv_b = f32(inputs["L0_attn_a_conv_b"])
    W_x = f32(inputs["L0_attn_a_W_x"])
    W_dt = f32(inputs["L0_attn_a_W_dt"])
    b_dt = f32(inputs["L0_attn_a_b_dt"])
    W_out = f32(inputs["L0_attn_a_W_out"])
    A_log = f32(inputs["L0_attn_a_A_log"])
    D_skip = f32(inputs["L0_attn_a_D"])
    W_write = f32(inputs["L0_attn_b_W_write"])
    W_read = f32(inputs["L0_attn_b_W_read"])
    w1_w = f32(inputs["L0_ffn_w1_w"]); w1_s = f32(inputs["L0_ffn_w1_s"])
    w2_w = f32(inputs["L0_ffn_w2_w"]); w2_s = f32(inputs["L0_ffn_w2_s"])
    h1_w = f32(inputs["L0_ffn_h1_w"]); h1_s = f32(inputs["L0_ffn_h1_s"])
    h2_w = f32(inputs["L0_ffn_h2_w"]); h2_s = f32(inputs["L0_ffn_h2_s"])
    w3_w = f32(inputs["L0_ffn_w3_w"]); w3_s = f32(inputs["L0_ffn_w3_s"])

    com = {}
    com["w_inT"] = _tile_w(W_in.T, 512)
    com["w_outT"] = _tile_w(W_out.T, 256)
    com["w_writeT"] = _tile_w(W_write.T, 512)
    com["w_readT"] = _tile_w(W_read.T, 512)
    com["w1T"] = _tile_w(w1_w.T, 512)
    com["w2T"] = _tile_w(w2_w.T, 512)
    com["h1T"] = _tile_w(h1_w.T, 128)
    com["h2T"] = _tile_w(h2_w.T, 128)
    com["w3T"] = _tile_w(w3_w.T, 128)
    def kp(W2, m):
        K = W2.shape[0]
        kt = K // 128
        return np.ascontiguousarray(W2.reshape(kt, 128, m).transpose(1, 0, 2).reshape(128, kt * m))
    com["wx_dtT"] = kp(W_x[0:DTR].T, DTR)
    com["wx_bT"] = kp(W_x[DTR:DTR + 16].T, 16)
    com["wx_cT"] = kp(W_x[DTR + 16:DTR + 32].T, 16)
    com["wdtT"] = np.ascontiguousarray(W_dt.T)
    com["conv_w_l"] = np.ascontiguousarray(conv_w.reshape(NDT, 128, 4).transpose(1, 0, 2).reshape(128, NDT * 4))
    com["conv_b_l"] = np.ascontiguousarray(conv_b.reshape(NDT, 128).T)
    com["D_l"] = np.ascontiguousarray(D_skip.reshape(NDT, 128).T)
    com["bdt_l"] = np.ascontiguousarray(b_dt.reshape(NDT, 128).T)
    com["Alog_l"] = np.ascontiguousarray(A_log.reshape(NDT, 128, DS).transpose(1, 0, 2).reshape(128, NDT * DS))
    com["scales_l"] = np.ascontiguousarray(np.concatenate([
        w1_s.reshape(NH, 128).T, w2_s.reshape(NH, 128).T, h1_s.reshape(NH, 128).T,
        h2_s.reshape(NH, 128).T, w3_s.reshape(NDM, 128).T], axis=1))
    com["sel16"] = np.repeat(np.eye(16, dtype=np.float32)[:, :, None], 128, 2).reshape(16, 2048)
    kk = np.arange(TC)
    karr = (np.arange(4)[:, None, None] * 128 + np.arange(128)[None, :, None])
    cgrid = kk[None, None, :]
    diffs = np.maximum(cgrid - 1 - karr, 0).astype(np.float32)
    causal = (cgrid > karr).astype(np.float32)
    com["diffs_l"] = np.ascontiguousarray(diffs.transpose(1, 0, 2).reshape(128, 4 * TC))
    com["causal_l"] = np.ascontiguousarray(causal.transpose(1, 0, 2).reshape(128, 4 * TC))
    com["pos_row"] = np.arange(TC, dtype=np.float32).reshape(1, TC)
    com["revpos_l"] = np.ascontiguousarray(
        (TC - 1 - (np.arange(4)[:, None] * 128 + np.arange(128)[None, :])).astype(np.float32).T)
    com["ident"] = np.eye(128, dtype=np.float32)
    com["decay_s"] = f32(inputs["L0_attn_b_decay"]).reshape(1, 1)
    com["logalpha_s"] = f32(inputs["L0_attn_b_log_alpha"]).reshape(1, 1)

    hs = f32(inputs["L0_attn_a_hidden_states"])
    x0s = []
    for bb in range(NCORE):
        xfm = hs[bb].T
        x0 = np.ascontiguousarray(
            xfm.reshape(NDM, 128, NCHUNK, TC).transpose(2, 1, 0, 3))[:nchunk]
        x0s.append(x0)
    return com, x0s


def kernel(**inputs):
    com, x0s = prep_inputs(inputs)
    nc = build_program()
    in_maps = [dict(com, x0=x0s[bb]) for bb in range(NCORE)]
    res = run_bass_kernel_spmd(nc, in_maps, core_ids=list(range(NCORE)))
    out = np.empty((NCORE, T, D), np.float32)
    for bb in range(NCORE):
        o = np.asarray(res.results[bb]["out_fm"], np.float32)
        xfm = o.transpose(2, 1, 0, 3).reshape(D, T)
        out[bb] = xfm.T
    return out


if __name__ == "__main__":
    pass


# version 21 (final)
# speedup vs baseline: 1.0897x; 1.0598x over previous
import sys
sys.path.insert(0, "/opt/trn_rl_repo")
import contextlib
import os
import numpy as np
import ml_dtypes

import concourse.bass as bass
import concourse.mybir as mybir
from concourse.bass_utils import run_bass_kernel_spmd

F32 = mybir.dt.float32
BF16 = mybir.dt.bfloat16
AF = mybir.ActivationFunctionType
ALU = mybir.AluOpType

NCORE = 8
T = 8192
D = 1024
DI = 2048
DS = 16
DTR = 64
TC = 512
HC = 256
NCHUNK = T // TC
NDT = DI // 128
NDM = D // 128
DH = 4096
NH = DH // 128

ENG_SEMS = {"sync": "SY", "tensor": "PE", "vector": "DV", "scalar": "AC", "gpsimd": "PL"}

WSPECS = {
    "w_inT":    (8, 8, 512),
    "w_outT":   (4, 16, 256),
    "w_writeT": (2, 8, 512),
    "w_readT":  (2, 8, 512),
    "w1T":      (8, 8, 512),
    "w2T":      (8, 8, 512),
    "h1T":      (32, 32, 128),
    "h2T":      (32, 32, 128),
    "w3T":      (8, 32, 128),
}


class Buf:
    __slots__ = ("ready", "users", "groups")
    def __init__(self):
        self.ready = {}
        self.users = {}
        self.groups = []


class Group:
    def __init__(self):
        self.tag = None
        self.users = {}
        self.barrier = {}


DMA_SEMS = (["RS0", "RS1", "RS2", "RS3", "RF0", "RF1",
             "MSD0", "MSD1", "MSU0", "MSU1", "MSZ0", "MSZ1",
             "SPU", "SPZ", "SPDT", "SPY", "AW0", "AW1", "WTS",
             "SCW", "SYC", "X0Q", "STQ", "SYQ", "GQQ",
             "CW0", "CW1", "CW2", "CW3", "CW4", "UG0", "UG1"])
FULL_WAIT = {"SPU", "SPZ", "SPDT", "SPY", "WTS", "SCW", "SYC", "X0Q"}


class Planner:
    def __init__(self):
        self.ops = {e: [] for e in ENG_SEMS}
        self.count = {s: 0 for s in list(ENG_SEMS.values()) + DMA_SEMS}

    def op(self, engine, emit, reads=(), writes=(), extra=(), dma_queue=None, defer=False):
        waits = {}
        def mw(s, v):
            if v > waits.get(s, -1):
                waits[s] = v
        switched = []
        for blist, iswrite in ((reads, False), (writes, True)):
            for bb in blist:
                if iswrite:
                    for s, v in bb.users.items():
                        mw(s, v)
                for s, v in bb.ready.items():
                    mw(s, v)
                for gr, tag in bb.groups:
                    if gr.tag != tag:
                        switched.append((gr, tag))
                    else:
                        for s, v in gr.barrier.items():
                            mw(s, v)
        for gr, tag in switched:
            nb = dict(gr.barrier)
            for s, v in gr.users.items():
                if v > nb.get(s, -1):
                    nb[s] = v
            for s, v in nb.items():
                mw(s, v)
            gr.barrier = nb
            gr.users = {}
            gr.tag = tag
        for t_ in extra:
            if t_ is not None:
                mw(t_[0], t_[1])
        for s in list(waits):
            if s in FULL_WAIT:
                waits[s] = self.count[s]
        if dma_queue is not None:
            sem = dma_queue
            self.count[sem] += 16
            tok = (sem, self.count[sem])
            inc = (sem, 16)
        elif defer:
            sem = ENG_SEMS[engine]
            tok = (sem, self.count[sem] + 1)
            inc = None
        else:
            sem = ENG_SEMS[engine]
            self.count[sem] += 1
            tok = (sem, self.count[sem])
            inc = (sem, 1)
        self.ops[engine].append((waits, emit, inc))
        for bb in reads:
            if tok[1] > bb.users.get(tok[0], -1):
                bb.users[tok[0]] = tok[1]
            for gr, _t in bb.groups:
                if tok[1] > gr.users.get(tok[0], -1):
                    gr.users[tok[0]] = tok[1]
        for bb in writes:
            bb.ready = {tok[0]: tok[1]}
            bb.users = {tok[0]: tok[1]}
            for gr, _t in bb.groups:
                if tok[1] > gr.users.get(tok[0], -1):
                    gr.users[tok[0]] = tok[1]
        return tok


NO_SCAN = False
SCAN_AS_MUL = False
GEN_LEVEL = 9
YIELD_XZ = False
YIELD_PRE = False
DRAIN_STAGE = 3
RACE_DET = True

def build_program(nchunk=NCHUNK, dbg=False):
    nc = bass.Bass("TRN2", target_bir_lowering=False, debug=False, detect_race_conditions=RACE_DET)
    es = contextlib.ExitStack()

    def din(name, shape, dt=F32):
        return nc.dram_tensor(name, shape, dt, kind="ExternalInput").ap()

    x0_in = din("x0", [nchunk, 128, NDM, TC])
    out_fm = nc.dram_tensor("out_fm", [nchunk, 128, NDM, TC], F32, kind="ExternalOutput").ap()

    w_f32, w_scr = {}, {}
    for nm, (ns, kt, ew) in WSPECS.items():
        w_f32[nm] = din(nm, [ns, 128, kt, ew])
        w_scr[nm] = nc.dram_tensor("scr_" + nm, [ns, 128, kt, ew], BF16, kind="Internal").ap()

    scr_sp = {}
    for nm in ("sdt", "su", "sz", "sy"):
        scr_sp[nm] = nc.dram_tensor("scr_" + nm, [NDT, 128, TC], BF16, kind="Internal").ap()
    scr_WT = nc.dram_tensor("scr_WT", [8, 128, 1024], F32, kind="Internal").ap()
    scr_WTb = nc.dram_tensor("scr_WTb", [8, 128, 1024], BF16, kind="Internal").ap()

    wxdt_in = din("wx_dtT", [128, NDT * DTR])
    wxb_in = din("wx_bT", [128, NDT * 16])
    wxc_in = din("wx_cT", [128, NDT * 16])
    wdtT_in = din("wdtT", [64, DI])
    convw_in = din("conv_w_l", [128, NDT * 4])
    convb_in = din("conv_b_l", [128, NDT])
    D_in = din("D_l", [128, NDT])
    bdt_in = din("bdt_l", [128, NDT])
    Alog_in = din("Alog_l", [128, NDT * DS])
    scales_in = din("scales_l", [128, 4 * NH + NDM])
    sel_in = din("sel16", [16, 16 * 128])
    diffs_in = din("diffs_l", [128, 4 * TC])
    causal_in = din("causal_l", [128, 4 * TC])
    pos_in = din("pos_row", [1, TC])
    revpos_in = din("revpos_l", [128, 4])
    ident_in = din("ident", [128, 128])
    decay_in = din("decay_s", [1, 1])
    logalpha_in = din("logalpha_s", [1, 1])

    dbg_o = {}
    if dbg == 1:
        for nm in ("dbg_x1", "dbg_x2", "dbg_x3"):
            dbg_o[nm] = nc.dram_tensor(nm, [nchunk, 128, NDM * TC], F32, kind="ExternalOutput").ap()
    if dbg == 2:
        for nm in ("dbg_sdt", "dbg_su", "dbg_sz", "dbg_sy"):
            dbg_o[nm] = nc.dram_tensor(nm, [NDT, 128, TC], BF16, kind="ExternalOutput").ap()
        dbg_o["dbg_wtb"] = nc.dram_tensor("dbg_wtb", [8, 128, 1024], BF16, kind="ExternalOutput").ap()
        dbg_o["dbg_wtf"] = nc.dram_tensor("dbg_wtf", [8, 128, 1024], F32, kind="ExternalOutput").ap()

    def salloc(name, shape, dt):
        return nc.alloc_sbuf_tensor(name, list(shape), dt)

    x_res = salloc("x_res", [128, NDM * TC], F32)
    x0_next = salloc("x0_next", [128, NDM * TC], F32)
    st_scan = salloc("st_scan", [128, NDT * DS], F32)
    st_ctail = salloc("st_ctail", [128, NDT * 3], BF16)
    st_wktail = salloc("st_wktail", [128, NDM], BF16)
    ring = [salloc(f"ring{i}", [128, 4096], BF16) for i in range(4)]
    k_M = salloc("k_M", [128, 4 * TC], BF16)
    k_gp = salloc("k_gp", [128, TC], BF16)
    k_gw = salloc("k_gw", [128, 4], F32)
    k_logg = salloc("k_logg", [128, 1], F32)
    k_gC = salloc("k_gC", [128, 1], F32)
    k_alpha = salloc("k_alpha", [128, 1], F32)
    c_convw = salloc("c_convw", [128, NDT * 4], F32)
    c_convb = salloc("c_convb", [128, NDT], F32)
    c_D = salloc("c_D", [128, NDT], F32)
    c_bdt = salloc("c_bdt", [128, NDT], F32)
    c_negA = salloc("c_negA", [128, NDT * DS], F32)
    c_scales = salloc("c_scales", [128, 4 * NH + NDM], F32)
    c_sel = salloc("c_sel", [16, 16 * 128], BF16)
    c_revpos = salloc("c_revpos", [128, 4], F32)
    c_ident = salloc("c_ident", [128, 128], BF16)
    c_ones_r = salloc("c_ones_r", [1, 128], F32)
    c_ones_c = salloc("c_ones_c", [128, 1], BF16)
    s_decay = salloc("s_decay", [1, 1], F32)
    s_logalpha = salloc("s_logalpha", [1, 1], F32)
    s_t1 = salloc("s_t1", [1, 1], F32)
    k_eps = salloc("k_eps", [128, 1], F32)
    k_half = salloc("k_half", [128, 1], F32)
    k_neghalf = salloc("k_neghalf", [128, 1], F32)
    k_invD = salloc("k_invD", [128, 1], F32)
    k_tc = salloc("k_tc", [128, 1], F32)
    l_m = salloc("l_m", [1, TC], F32)
    l_v = salloc("l_v", [1, TC], F32)
    l_r = salloc("l_r", [1, TC], F32)
    l_mr = salloc("l_mr", [1, TC], F32)
    l_rrep = salloc("l_rrep", [128, TC], F32)
    l_mrrep = salloc("l_mrrep", [128, TC], F32)
    lt_bf = [salloc("lt_bf0", [128, TC], BF16)]
    wxdt_b = salloc("wxdt_b", [128, NDT * DTR], BF16)
    wxb_b = salloc("wxb_b", [128, NDT * 16], BF16)
    wxc_b = salloc("wxc_b", [128, NDT * 16], BF16)
    wdt_b = salloc("wdt_b", [64, DI], BF16)

    ARENA_BYTES = (nc.sbuf_bytes_remaining - 256) // 32 * 32
    arena_h = nc.alloc_sbuf_tensor("arena", [128, ARENA_BYTES], mybir.dt.uint8)
    AB = nc.lookup_mloc(arena_h).addr
    assert AB is not None

    def aalloc(cur2, name, shape, dt):
        per = int(np.prod(shape[1:])) * mybir.dt.size(dt)
        off = (cur2[0] + 31) // 32 * 32
        cur2[0] = off + per
        assert cur2[0] <= ARENA_BYTES, f"arena overflow {name}: {cur2[0]} > {ARENA_BYTES}"
        return nc.alloc_sbuf_tensor_at(name, list(shape), dt, offset=AB + off)

    am = [0]
    mb_pad = aalloc(am, "mb_pad", [128, TC + 3], BF16)
    mb_ev = [aalloc(am, f"mb_ev{i}", [128, TC], BF16) for i in range(3)]
    ms_dt = [aalloc(am, f"ms_dt{i}", [128, HC], BF16) for i in range(2)]
    ms_u = [aalloc(am, f"ms_u{i}", [128, HC], BF16) for i in range(2)]
    ms_z = [aalloc(am, f"ms_z{i}", [128, HC], BF16) for i in range(2)]
    ms_y = [aalloc(am, f"ms_y{i}", [128, HC], BF16) for i in range(2)]
    m_dtraw = aalloc(am, "m_dtraw", [64, TC], BF16)
    m_b = aalloc(am, "m_b", [16, TC], BF16)
    m_c = aalloc(am, "m_c", [16, TC], BF16)
    xb_off = (am[0] + 31) // 32 * 32
    m_xln = nc.alloc_sbuf_tensor_at("m_xln", [128, NDM * TC], BF16, offset=AB + xb_off)
    m_Ball = nc.alloc_sbuf_tensor_at("m_Ball", [128, DS * HC], BF16, offset=AB + xb_off)
    am[0] = xb_off + DS * HC * 2
    m_Call = aalloc(am, "m_Call", [128, DS * HC], BF16)
    m_dtu2 = [aalloc(am, f"m_dtu2{i}", [128, HC], BF16) for i in range(2)]
    m_dA = [aalloc(am, f"m_dA{i}", [128, HC], F32) for i in range(8)]
    m_dBu = [aalloc(am, f"m_dBu{i}", [128, HC], BF16) for i in range(3)]
    m_h = [aalloc(am, f"m_h{i}", [128, HC], BF16) for i in range(3)]
    m_yacc = aalloc(am, "m_yacc", [128, HC], F32)
    m_ytmp = aalloc(am, "m_ytmp", [128, HC], BF16)
    m_t1 = aalloc(am, "m_t1", [128, TC], F32)
    m_t2 = aalloc(am, "m_t2", [128, HC], BF16)
    MS = (am[0] + 31) // 32 * 32

    ap_ = [MS]
    ring_f = [aalloc(ap_, "ringf0", [128, 2048], F32)]
    p_diffs = aalloc(ap_, "p_diffs", [128, 4 * TC], F32)
    p_causal = aalloc(ap_, "p_causal", [128, 4 * TC], F32)
    p_wf1 = aalloc(ap_, "p_wf1", [128, NDT * DTR], F32)
    p_wf2 = aalloc(ap_, "p_wf2", [128, NDT * 16], F32)
    p_wf3 = aalloc(ap_, "p_wf3", [128, NDT * 16], F32)
    p_wf4 = aalloc(ap_, "p_wf4", [64, DI], F32)
    c_pos = aalloc(ap_, "c_pos", [1, TC], F32)
    s_row = aalloc(ap_, "s_row", [1, TC], F32)

    af = [MS]
    f_h = aalloc(af, "f_h", [128, NH * TC], BF16)
    fg_off = (af[0] + 31) // 32 * 32
    f_g = nc.alloc_sbuf_tensor_at("f_g", [128, NH * TC], BF16, offset=AB + fg_off)
    f_xln = nc.alloc_sbuf_tensor_at("f_xln", [128, NDM * TC], BF16, offset=AB + fg_off)
    af[0] = fg_off + NH * TC * 2
    f_t1 = aalloc(af, "f_t1", [128, TC], F32)
    ffn_end = af[0]

    aa = [MS]
    a_rkpad = aalloc(aa, "a_rkpad", [128, NDM * (TC + 1)], BF16)
    a_vtm = aalloc(aa, "a_vtm", [128, 4 * 1024], BF16)
    a_wktm = aalloc(aa, "a_wktm", [128, 4 * 1024], BF16)
    a_SmT = aalloc(aa, "a_SmT", [128, 4 * TC], BF16)
    a_reads = aalloc(aa, "a_reads", [128, NDM * TC], BF16)
    aw_f32 = [aalloc(aa, f"aw_f32{i}", [128, 1024], F32) for i in range(4)]
    aw_bf = aalloc(aa, "aw_bf", [128, 1024], BF16)
    a_t1 = aalloc(aa, "a_t1", [128, TC], F32)
    attn_end = aa[0]

    arena_end = max(ap_[0], ffn_end, attn_end)
    if os.environ.get("ARENA_DEBUG"):
        print(f"ARENA: bytes={ARENA_BYTES} MS={MS} ap={ap_[0]} ffn={ffn_end} attn={attn_end} end={arena_end} slack={ARENA_BYTES-arena_end}")
    assert arena_end <= ARENA_BYTES, f"SBUF over: {arena_end} > {ARENA_BYTES}"

    pb = [nc.alloc_psum_tensor(f"pb{i}", [128, 512], F32) for i in range(6)]
    pb_dw = nc.alloc_psum_tensor("pb_dw", [128, 512], F32)
    pbT = nc.alloc_psum_tensor("pbT", [128, 256], BF16)

    sems = {}
    for s in ["SY", "PE", "DV", "AC", "PL"] + DMA_SEMS:
        sems[s] = es.enter_context(nc.semaphore(s))
    pl = Planner()
    G = Group()
    G2 = Group()
    G3 = Group()
    BUFS = {}

    def b(h):
        k = id(h)
        if k not in BUFS:
            BUFS[k] = Buf()
        return BUFS[k]

    def gb(h, tag, grp=None):
        bb = b(h)
        gr = G if grp is None else grp
        found = False
        for idx, (g0, _t) in enumerate(bb.groups):
            if g0 is gr:
                bb.groups[idx] = (gr, tag)
                found = True
        if not found:
            bb.groups.append((gr, tag))
        return bb

    SBUFS = {}
    COARSE = bool(os.environ.get("KB_COARSE"))
    _fine_env = os.environ.get("KB_FINE")
    FINE_ONLY = set(_fine_env.split(",")) if _fine_env else None

    def sbl(h, n):
        name = nc.lookup_mloc(h).name
        if COARSE or (FINE_ONLY is not None and name not in FINE_ONLY):
            return [b(h)] * n
        k = (id(h), n)
        if k not in SBUFS:
            SBUFS[k] = [Buf() for _ in range(n)]
        return SBUFS[k]

    def tagb(bb, tag, grp):
        found = False
        for idx, (g0, _t) in enumerate(bb.groups):
            if g0 is grp:
                bb.groups[idx] = (grp, tag)
                found = True
        if not found:
            bb.groups.append((grp, tag))
        return bb

    pb4h = [b(pb[4])]
    pb5h = [b(pb[5])]
    pbTh = [b(pbT)]

    scr_ready = Buf()
    st_bufs = [Buf() for _ in range(NDT * DS)]
    sp_bufs = {nm: [Buf() for _ in range(NDT)] for nm in ("sdt", "su", "sz", "sy")}
    wt_bufs = [Buf() for _ in range(8)]
    wtb_bufs = [Buf() for _ in range(8)]

    def sy_dma(dst, src, reads=(), writes=(), queue="SYQ"):
        return pl.op("sync", lambda e, d=dst, s=src: e.dma_start(out=d, in_=s),
                     reads=reads, writes=writes, dma_queue=queue)

    def gq_dma(dst, src, reads=(), writes=(), queue="GQQ"):
        return pl.op("gpsimd", lambda e, d=dst, s=src: e.dma_start(out=d, in_=s),
                     reads=reads, writes=writes, dma_queue=queue)

    def sc_dma(dst, src, reads=(), writes=(), queue="GQQ"):
        return pl.op("scalar", lambda e, d=dst, s=src: e.dma_start(out=d, in_=s),
                     reads=reads, writes=writes, dma_queue=queue)

    def dve(emit, reads=(), writes=()):
        return pl.op("vector", emit, reads=reads, writes=writes)

    def act(emit, reads=(), writes=()):
        return pl.op("scalar", emit, reads=reads, writes=writes)

    def pool(emit, reads=(), writes=()):
        return pl.op("gpsimd", emit, reads=reads, writes=writes)

    def pe(emit, reads=(), writes=()):
        return pl.op("tensor", emit, reads=reads, writes=writes)

    def mm_group(psum_ap, psum_buf, pairs):
        n = len(pairs)
        for i, (lt, rh, rbs) in enumerate(pairs):
            pl.op("tensor",
                  lambda e, o=psum_ap, l=lt, r=rh, st=(i == 0), sp=(i == n - 1):
                  e.matmul(o, l, r, start=st, stop=sp),
                  reads=list(rbs), writes=[psum_buf], defer=(i < n - 1))

    for dst, src, tagged in [
            (c_convw, convw_in, 0), (c_convb, convb_in, 0), (c_D, D_in, 0),
            (c_bdt, bdt_in, 0), (c_negA, Alog_in, 0), (c_scales, scales_in, 0),
            (c_pos, pos_in, 1), (c_revpos, revpos_in, 0),
            (s_decay, decay_in, 0), (s_logalpha, logalpha_in, 0),
            (p_diffs, diffs_in, 1), (p_causal, causal_in, 1),
            (p_wf1, wxdt_in, 1), (p_wf2, wxb_in, 1), (p_wf3, wxc_in, 1),
            (p_wf4, wdtT_in, 1)]:
        sy_dma(dst[:], src, writes=[gb(dst, "pro") if tagged else b(dst)], queue="SYC")
    sy_dma(ring_f[0][:, 0:128], ident_in, writes=[gb(ring_f[0], "pro")], queue="RF0")
    dve(lambda e: e.tensor_copy(c_ident[:], ring_f[0][:, 0:128]),
        reads=[gb(ring_f[0], "pro")], writes=[b(c_ident)])
    sy_dma(ring_f[0][0:16, 0:2048], sel_in, writes=[gb(ring_f[0], "pro")], queue="RF0")
    dve(lambda e: e.tensor_copy(c_sel[:], ring_f[0][0:16, 0:2048]),
        reads=[gb(ring_f[0], "pro")], writes=[b(c_sel)])
    dve(lambda e: e.memset(c_ones_r[:], 1.0), writes=[b(c_ones_r)])
    dve(lambda e: e.memset(c_ones_c[:], 1.0), writes=[b(c_ones_c)])
    dve(lambda e: e.memset(k_eps[:], 1e-5), writes=[b(k_eps)])
    dve(lambda e: e.memset(k_half[:], 0.5), writes=[b(k_half)])
    dve(lambda e: e.memset(k_neghalf[:], -0.5), writes=[b(k_neghalf)])
    dve(lambda e: e.memset(k_invD[:], 1.0 / D), writes=[b(k_invD)])
    dve(lambda e: e.memset(k_tc[:], float(TC)), writes=[b(k_tc)])
    dve(lambda e: e.tensor_copy(wxdt_b[:], p_wf1[:]), reads=[gb(p_wf1, "pro")], writes=[b(wxdt_b)])
    dve(lambda e: e.tensor_copy(wxb_b[:], p_wf2[:]), reads=[gb(p_wf2, "pro")], writes=[b(wxb_b)])
    dve(lambda e: e.tensor_copy(wxc_b[:], p_wf3[:]), reads=[gb(p_wf3, "pro")], writes=[b(wxc_b)])
    dve(lambda e: e.tensor_copy(wdt_b[:], p_wf4[:]), reads=[gb(p_wf4, "pro")], writes=[b(wdt_b)])
    act(lambda e: e.activation(c_negA[:], c_negA[:], AF.Exp), reads=[b(c_negA)], writes=[b(c_negA)])
    dve(lambda e: e.tensor_scalar_mul(c_negA[:], c_negA[:], -1.0), reads=[b(c_negA)], writes=[b(c_negA)])
    act(lambda e: e.activation(s_t1[:], s_decay[:], AF.Sigmoid), reads=[b(s_decay)], writes=[b(s_t1)])
    act(lambda e: e.activation(s_t1[:], s_t1[:], AF.Ln), reads=[b(s_t1)], writes=[b(s_t1)])
    pe(lambda e: e.matmul(pb[4][:, 0:1], c_ones_r[:], s_t1[:], start=True, stop=True),
       reads=[b(c_ones_r), b(s_t1)], writes=pb4h)
    act(lambda e: e.copy(k_logg[:], pb[4][:, 0:1]), reads=pb4h, writes=[b(k_logg)])
    act(lambda e: e.activation(k_gC[:], k_logg[:], AF.Exp, scale=k_tc[:, 0:1]),
        reads=[b(k_logg), b(k_tc)], writes=[b(k_gC)])
    act(lambda e: e.activation(s_t1[:], s_logalpha[:], AF.Exp),
        reads=[b(s_logalpha)], writes=[b(s_t1)])
    pe(lambda e: e.matmul(pb[4][:, 0:1], c_ones_r[:], s_t1[:], start=True, stop=True),
       reads=[b(c_ones_r), b(s_t1)], writes=pb4h)
    act(lambda e: e.copy(k_alpha[:], pb[4][:, 0:1]), reads=pb4h, writes=[b(k_alpha)])
    for kk in range(4):
        act(lambda e, kk=kk: e.activation(k_M[:, kk * TC:(kk + 1) * TC],
                                          p_diffs[:, kk * TC:(kk + 1) * TC],
                                          AF.Exp, scale=k_logg[:, 0:1]),
            reads=[gb(p_diffs, "pro"), b(k_logg)], writes=[b(k_M)])
    dve(lambda e: e.tensor_mul(k_M[:], k_M[:], p_causal[:]),
        reads=[b(k_M), gb(p_causal, "pro")], writes=[b(k_M)])
    act(lambda e: e.activation(s_row[:], c_pos[:], AF.Exp, scale=k_logg[0:1, 0:1]),
        reads=[gb(c_pos, "pro"), b(k_logg)], writes=[gb(s_row, "pro")])
    pe(lambda e: e.matmul(pb[4][:, :], c_ones_r[:], s_row[:], start=True, stop=True),
       reads=[b(c_ones_r), gb(s_row, "pro")], writes=pb4h)
    act(lambda e: e.copy(k_gp[:], pb[4][:, :]), reads=pb4h, writes=[b(k_gp)])
    act(lambda e: e.activation(k_gw[:], c_revpos[:], AF.Exp, scale=k_logg[:, 0:1]),
        reads=[b(c_revpos), b(k_logg)], writes=[b(k_gw)])
    dve(lambda e: e.memset(ring_f[0][:, :], 0.0), writes=[gb(ring_f[0], "pro")])
    dve(lambda e: e.memset(ring[0][:, 0:1024], 0.0), writes=[b(ring[0])])
    for jt in range(8):
        gq_dma(scr_WT[jt], ring_f[0][:, 0:1024], reads=[gb(ring_f[0], "pro")],
               writes=[wt_bufs[jt]], queue="WTS")
        gq_dma(scr_WTb[jt], ring[0][:, 0:1024], reads=[b(ring[0])], writes=[wtb_bufs[jt]],
               queue="WTS")
    tok_st = dve(lambda e: e.memset(st_scan[:], 0.0), writes=[b(st_scan)])
    for sb_ in st_bufs:
        sb_.ready = {tok_st[0]: tok_st[1]}
    dve(lambda e: e.memset(st_ctail[:], 0.0), writes=[b(st_ctail)])
    dve(lambda e: e.memset(st_wktail[:], 0.0), writes=[b(st_wktail)])

    fam_ready = {}
    for a in range(WSPECS["w_inT"][0]):
        gq_dma(w_scr["w_inT"][a], w_f32["w_inT"][a], queue="CW%d" % (a % 2))
    fr_in = Buf()
    fr_in.ready = {"CW0": pl.count["CW0"], "CW1": pl.count["CW1"]}
    fam_ready["w_inT"] = fr_in
    rest_qs = ("CW2", "CW3", "CW4")
    ci = 0
    for nm in ["w1T", "w2T", "h1T", "h2T", "w3T", "w_outT", "w_writeT", "w_readT"]:
        ns, kt, ew = WSPECS[nm]
        for a in range(ns):
            gq_dma(w_scr[nm][a], w_f32[nm][a], queue=rest_qs[ci % 3])
            ci += 1
    fr_all = Buf()
    fr_all.ready = {q: pl.count[q] for q in rest_qs}
    for nm in ["w1T", "w2T", "h1T", "h2T", "w3T", "w_outT", "w_writeT", "w_readT"]:
        fam_ready[nm] = fr_all

    gq_dma(x0_next[:], x0_in[0].rearrange("p j t -> p (j t)"), writes=[b(x0_next)], queue="X0Q")

    rstate = [0]
    def load_wslice(nm, s, ring_set=None):
        ns, kt, ew = WSPECS[nm]
        if ring_set is None:
            si = rstate[0] % 4
        else:
            si = ring_set[rstate[0] % len(ring_set)]
        rstate[0] += 1
        slot = ring[si]
        n = kt * ew
        sy_dma(slot[:, 0:n], w_scr[nm][s].rearrange("p k e -> p (k e)"),
               reads=[fam_ready[nm]], writes=[b(slot)], queue="RS%d" % si)
        return slot

    def layernorm(t1buf, t1b, dest, dest_bufs, cp_t, sq_t, src_t, src_bufs,
                  fold=False, fold_bufs=None):
        for j in range(NDM):
            sl = slice(j * TC, (j + 1) * TC)
            ca, cb = cp_t(j)
            dve(lambda e, sl=sl, ca=ca: e.tensor_copy(ca, src_t[:, sl]),
                reads=[src_bufs[j]], writes=[cb])
            pe(lambda e, ca=ca, st=(j == 0), sp=(j == NDM - 1):
               e.matmul(pb[4][0:1, :], c_ones_c[:], ca, start=st, stop=sp),
               reads=[cb, b(c_ones_c)], writes=pb4h)
        for j in range(NDM):
            sl = slice(j * TC, (j + 1) * TC)
            sa, sb_ = sq_t(j)
            act(lambda e, sl=sl, sa=sa: e.activation(sa, src_t[:, sl], AF.Square),
                reads=[src_bufs[j]], writes=[sb_])
            pe(lambda e, sa=sa, st=(j == 0), sp=(j == NDM - 1):
               e.matmul(pb[5][0:1, :], c_ones_c[:], sa, start=st, stop=sp),
               reads=[sb_, b(c_ones_c)], writes=pb5h)
        act(lambda e: e.activation(l_m[:], pb[4][0:1, :], AF.Identity, scale=k_invD[0:1, 0:1]),
            reads=pb4h + [b(k_invD)], writes=[b(l_m)])
        act(lambda e: e.activation(l_v[:], pb[5][0:1, :], AF.Identity, scale=k_invD[0:1, 0:1]),
            reads=pb5h + [b(k_invD)], writes=[b(l_v)])
        act(lambda e: e.activation(l_mr[:], l_m[:], AF.Square), reads=[b(l_m)], writes=[b(l_mr)])
        dve(lambda e: e.tensor_sub(l_v[:], l_v[:], l_mr[:]), reads=[b(l_v), b(l_mr)], writes=[b(l_v)])
        act(lambda e: e.activation(l_r[:], l_v[:], AF.Ln, bias=k_eps[0:1, 0:1]),
            reads=[b(l_v), b(k_eps)], writes=[b(l_r)])
        act(lambda e: e.activation(l_r[:], l_r[:], AF.Exp, scale=k_neghalf[0:1, 0:1]),
            reads=[b(l_r), b(k_neghalf)], writes=[b(l_r)])
        dve(lambda e: e.tensor_mul(l_mr[:], l_m[:], l_r[:]), reads=[b(l_m), b(l_r)], writes=[b(l_mr)])
        pe(lambda e: e.matmul(pb[4][:, :], c_ones_r[:], l_r[:], start=True, stop=True),
           reads=[b(l_r), b(c_ones_r)], writes=pb4h)
        act(lambda e: e.copy(l_rrep[:], pb[4][:, :]), reads=pb4h, writes=[b(l_rrep)])
        pe(lambda e: e.matmul(pb[5][:, :], c_ones_r[:], l_mr[:], start=True, stop=True),
           reads=[b(l_mr), b(c_ones_r)], writes=pb5h)
        act(lambda e: e.copy(l_mrrep[:], pb[5][:, :]), reads=pb5h, writes=[b(l_mrrep)])
        for j in range(NDM):
            sl = slice(j * TC, (j + 1) * TC)
            dve(lambda e, sl=sl: e.tensor_mul(t1buf[:], src_t[:, sl], l_rrep[:]),
                reads=[src_bufs[j], b(l_rrep)], writes=[t1b])
            dve(lambda e, d=dest(j): e.tensor_sub(d, t1buf[:], l_mrrep[:]),
                reads=[t1b, b(l_mrrep)], writes=[dest_bufs[j]])
            if fold:
                dve(lambda e, sl=sl: e.tensor_add(x_res[:, sl], x_res[:, sl], t1buf[:]),
                    reads=[t1b, fold_bufs[j]], writes=[fold_bufs[j]])
                dve(lambda e, sl=sl: e.tensor_sub(x_res[:, sl], x_res[:, sl], l_mrrep[:]),
                    reads=[b(l_mrrep)], writes=[fold_bufs[j]])

    def ffn(tag, adv):
        fxs = sbl(f_xln, NDM)
        fhs = sbl(f_h, NH)
        fgs = sbl(f_g, NH)
        xrs = sbl(x_res, NDM)
        for j in range(NDM):
            tagb(fxs[j], tag, G)
            tagb(fxs[j], "x" + tag, G3)
        for et in range(NH):
            tagb(fhs[et], tag, G)
            tagb(fgs[et], tag, G)
            tagb(fgs[et], "g" + tag, G3)
        ft1 = tagb(b(f_t1), tag, G)
        layernorm(f_t1, ft1, lambda j: f_xln[:, j * TC:(j + 1) * TC], fxs,
                  lambda j: (f_xln[:, j * TC:(j + 1) * TC], fxs[j]),
                  lambda j: (f_h[:, j * TC:(j + 1) * TC], fhs[j]),
                  x_res, xrs)
        for s in range(8):
            slot1 = load_wslice("w1T", s)
            slot2 = load_wslice("w2T", s)
            for m in range(4):
                et = 4 * s + m
                p1, p2 = pb[(2 * m) % 6], pb[(2 * m + 1) % 6]
                mm_group(p1[:, :], b(p1),
                         [(slot1[:, k * 512 + m * 128: k * 512 + (m + 1) * 128],
                           f_xln[:, k * TC:(k + 1) * TC], [b(slot1), fxs[k]]) for k in range(NDM)])
                mm_group(p2[:, :], b(p2),
                         [(slot2[:, k * 512 + m * 128: k * 512 + (m + 1) * 128],
                           f_xln[:, k * TC:(k + 1) * TC], [b(slot2), fxs[k]]) for k in range(NDM)])
                act(lambda e, p=p1, et=et: e.activation(f_t1[:], p[:, :], AF.Silu,
                                                        scale=c_scales[:, et:et + 1]),
                    reads=[b(p1), b(c_scales)], writes=[ft1])
                dve(lambda e, p=p2, et=et: e.scalar_tensor_tensor(
                        f_h[:, et * TC:(et + 1) * TC], p[:, :],
                        c_scales[:, NH + et:NH + et + 1], f_t1[:], ALU.mult, ALU.mult),
                    reads=[b(p2), b(c_scales), ft1], writes=[fhs[et]])
            adv(1.0)
        for wnm, src, srcbl, dst, dstbl, so in (("h1T", f_h, fhs, f_g, fgs, 2 * NH),
                                                ("h2T", f_g, fgs, f_h, fhs, 3 * NH)):
            def gelu_evac(et, p):
                act(lambda e, p=p, et=et, so=so, dst=dst: e.activation(
                        dst[:, et * TC:(et + 1) * TC], p[:, :], AF.Gelu,
                        scale=c_scales[:, so + et:so + et + 1]),
                    reads=[b(p), b(c_scales)], writes=[dstbl[et]])
            for s in range(32):
                slot = load_wslice(wnm, s)
                p = pb[s % 6]
                mm_group(p[:, :], b(p),
                         [(slot[:, k * 128: (k + 1) * 128],
                           src[:, k * TC:(k + 1) * TC], [b(slot), srcbl[k]]) for k in range(NH)])
                if s % 2 == 1:
                    gelu_evac(s - 1, pb[(s - 1) % 6])
                    gelu_evac(s, p)
                    adv(1.5)
        for s in range(8):
            slot = load_wslice("w3T", s)
            et = s
            p = pb[s % 4]
            mm_group(p[:, :], b(p),
                     [(slot[:, k * 128: (k + 1) * 128],
                       f_h[:, k * TC:(k + 1) * TC], [b(slot), fhs[k]]) for k in range(NH)])
            dve(lambda e, p=p, et=et: e.scalar_tensor_tensor(
                    x_res[:, et * TC:(et + 1) * TC], p[:, :],
                    c_scales[:, 4 * NH + et:4 * NH + et + 1],
                    x_res[:, et * TC:(et + 1) * TC], ALU.mult, ALU.add),
                reads=[b(p), b(c_scales), xrs[et]], writes=[xrs[et]])
            adv(0.5)

    def mamba_pre(c):
        mt1 = b(m_t1)
        mpad = b(mb_pad)
        mxs = sbl(m_xln, NDM)
        for j in range(NDM):
            tagb(mxs[j], f"x{c}", G2)
        x0b = [b(x0_next)] * NDM
        layernorm(m_t1, mt1, lambda j: m_xln[:, j * TC:(j + 1) * TC], mxs,
                  lambda j: (m_xln[:, j * TC:(j + 1) * TC], mxs[j]),
                  lambda j: (mb_ev[j % 3][:], b(mb_ev[j % 3])),
                  x0_next, x0b)
        for s in range(8):
            slot = load_wslice("w_inT", s)
            for m in range(4):
                et = 4 * s + m
                p = pb[et % 4]
                mm_group(p[:, :], b(p),
                         [(slot[:, k * 512 + m * 128: k * 512 + (m + 1) * 128],
                           m_xln[:, k * TC:(k + 1) * TC], [b(slot), mxs[k]]) for k in range(NDM)])
                if et < NDT:
                    i = et
                    dve(lambda e, i=i: e.tensor_copy(mb_pad[:, 0:3], st_ctail[:, i * 3: i * 3 + 3]),
                        reads=[b(st_ctail)], writes=[mpad])
                    dve(lambda e, p=p: e.tensor_scalar(mb_pad[:, 3:3 + TC], p[:, :],
                                                       0.0, 1.0, ALU.max, ALU.min),
                        reads=[b(p)], writes=[mpad])
                    dve(lambda e, i=i: e.tensor_copy(st_ctail[:, i * 3: i * 3 + 3],
                                                     mb_pad[:, TC: TC + 3]),
                        reads=[mpad], writes=[b(st_ctail)])
                    dve(lambda e, i=i: e.tensor_scalar(
                            m_t1[:], mb_pad[:, 0:TC], c_convw[:, 4 * i:4 * i + 1], None, ALU.mult),
                        reads=[mpad, b(c_convw)], writes=[mt1])
                    for q in (1, 2, 3):
                        dve(lambda e, i=i, q=q: e.scalar_tensor_tensor(
                                m_t1[:], mb_pad[:, q:q + TC],
                                c_convw[:, 4 * i + q:4 * i + q + 1], m_t1[:], ALU.mult, ALU.add),
                            reads=[mpad, b(c_convw), mt1], writes=[mt1])
                    dve(lambda e, i=i: e.tensor_scalar(
                            m_t1[:], m_t1[:], c_convb[:, i:i + 1], None, ALU.add),
                        reads=[b(c_convb), mt1], writes=[mt1])
                    act(lambda e: e.activation(m_t1[:], m_t1[:], AF.Silu),
                        reads=[mt1], writes=[mt1])
                    ev = mb_ev[i % 3]
                    dve(lambda e, ev=ev: e.tensor_scalar(ev[:], m_t1[:], 0.0, 1.0,
                                                         ALU.max, ALU.min),
                        reads=[mt1], writes=[b(ev)])
                    gq_dma(scr_sp["su"][i], ev[:], reads=[b(ev)], writes=[sp_bufs["su"][i]],
                           queue="SPU")
                    if YIELD_XZ:
                        yield
                else:
                    i = et - NDT
                    ev = mb_ev[i % 3]
                    dve(lambda e, p=p, ev=ev: e.tensor_scalar(ev[:], p[:, :],
                                                              0.0, 1.0, ALU.max, ALU.min),
                        reads=[b(p)], writes=[b(ev)])
                    gq_dma(scr_sp["sz"][i], ev[:], reads=[b(ev)], writes=[sp_bufs["sz"][i]],
                           queue="SPZ")
                    if YIELD_XZ:
                        yield
        uslots = []
        for g2_ in range(2):
            slot = ring[g2_]
            sy_dma(slot[:, :].rearrange("p (a t) -> p a t", a=8),
                   scr_sp["su"][8 * g2_:8 * g2_ + 8].rearrange("a p t -> p a t"),
                   reads=[sp_bufs["su"][8 * g2_ + q2] for q2 in range(8)], writes=[b(slot)],
                   queue="RS%d" % g2_)
            uslots.append(slot)
        def uview(k):
            return uslots[k // 8][:, (k % 8) * TC:(k % 8 + 1) * TC]
        mm_group(pb[0][0:64, :], b(pb[0]),
                 [(wxdt_b[:, k * DTR:(k + 1) * DTR], uview(k),
                   [b(wxdt_b), b(uslots[k // 8])]) for k in range(NDT)])
        dve(lambda e: e.tensor_scalar(m_dtraw[:], pb[0][0:64, :], 0.0, 1.0, ALU.max, ALU.min),
            reads=[b(pb[0])], writes=[b(m_dtraw)])
        mm_group(pb[1][0:16, :], b(pb[1]),
                 [(wxb_b[:, k * 16:(k + 1) * 16], uview(k),
                   [b(wxb_b), b(uslots[k // 8])]) for k in range(NDT)])
        dve(lambda e: e.tensor_copy(m_b[:], pb[1][0:16, :]), reads=[b(pb[1])], writes=[b(m_b)])
        mm_group(pb[2][0:16, :], b(pb[2]),
                 [(wxc_b[:, k * 16:(k + 1) * 16], uview(k),
                   [b(wxc_b), b(uslots[k // 8])]) for k in range(NDT)])
        dve(lambda e: e.tensor_copy(m_c[:], pb[2][0:16, :]), reads=[b(pb[2])], writes=[b(m_c)])
        if YIELD_PRE:
            yield
        for i in range(NDT):
            if YIELD_PRE and i % 2 == 0:
                yield
            p = pb[i % 4]
            pe(lambda e, i=i, p=p: e.matmul(p[:, :], wdt_b[:, i * 128:(i + 1) * 128],
                                            m_dtraw[:], start=True, stop=True),
               reads=[b(wdt_b), b(m_dtraw)], writes=[b(p)])
            act(lambda e, i=i, p=p: e.activation(m_t1[:], p[:, :], AF.Exp,
                                                 bias=c_bdt[:, i:i + 1]),
                reads=[b(p), b(c_bdt)], writes=[mt1])
            act(lambda e: e.activation(m_t1[:], m_t1[:], AF.Ln, bias=1.0),
                reads=[mt1], writes=[mt1])
            ev = mb_ev[i % 3]
            dve(lambda e, ev=ev: e.tensor_scalar(ev[:], m_t1[:], 1e-4, 20.0, ALU.max, ALU.min),
                reads=[mt1], writes=[b(ev)])
            gq_dma(scr_sp["sdt"][i], ev[:], reads=[b(ev)], writes=[sp_bufs["sdt"][i]],
                   queue="SPDT")

    def scan_gen(c):
        mballs = sbl(m_Ball, DS)
        mcalls = sbl(m_Call, DS)
        for n in range(DS):
            tagb(mballs[n], f"bc{c}", G2)
        myacc, mytmp = b(m_yacc), b(m_ytmp)
        mt2 = b(m_t2)
        def prefetch_scan(i, off, ri):
            for nm, bufs, qs in (("sdt", ms_dt, "MSD"), ("su", ms_u, "MSU"), ("sz", ms_z, "MSZ")):
                sc_dma(bufs[ri][:], scr_sp[nm][i][:, off:off + HC],
                       reads=[sp_bufs[nm][i]], writes=[b(bufs[ri])], queue=qs + str(ri))
        for hf in range(2):
            off = hf * HC
            for n in range(DS):
                pe(lambda e, n=n, off=off: e.matmul(
                        pb[4][:, 0:HC], c_sel[:, n * 128:(n + 1) * 128],
                        m_b[:, off:off + HC], start=True, stop=True),
                   reads=[b(c_sel), b(m_b)], writes=pb4h)
                act(lambda e, n=n: e.activation(m_Ball[:, n * HC:(n + 1) * HC],
                                                pb[4][:, 0:HC], AF.Identity),
                    reads=pb4h, writes=[mballs[n]])
                pe(lambda e, n=n, off=off: e.matmul(
                        pb[5][:, 0:HC], c_sel[:, n * 128:(n + 1) * 128],
                        m_c[:, off:off + HC], start=True, stop=True),
                   reads=[b(c_sel), b(m_c)], writes=pb5h)
                act(lambda e, n=n: e.activation(m_Call[:, n * HC:(n + 1) * HC],
                                                pb[5][:, 0:HC], AF.Identity),
                    reads=pb5h, writes=[mcalls[n]])
            if hf == 0 and GEN_LEVEL >= 1:
                prefetch_scan(0, 0, 0)
            for i in range(NDT):
                ri = i % 2
                sdt_b, su_b, sz_b = ms_dt[ri], ms_u[ri], ms_z[ri]
                if GEN_LEVEL >= 1:
                    if i + 1 < NDT:
                        prefetch_scan(i + 1, off, (i + 1) % 2)
                    elif hf == 0:
                        prefetch_scan(0, HC, 0)
                if GEN_LEVEL <= 1:
                    yield
                    continue
                dtu = m_dtu2[i % 2]
                dve(lambda e, dtu=dtu, sdt_b=sdt_b, su_b=su_b: e.tensor_mul(
                        dtu[:], sdt_b[:], su_b[:]),
                    reads=[b(sdt_b), b(su_b)], writes=[b(dtu)])
                def emit_dA(n):
                    g = i * DS + n
                    dA = m_dA[g % 8]
                    act(lambda e, g=g, dA=dA, sdt_b=sdt_b: e.activation(
                            dA[:], sdt_b[:], AF.Exp, scale=c_negA[:, g:g + 1]),
                        reads=[b(sdt_b), b(c_negA)], writes=[b(dA)])
                def emit_dBu(n):
                    g = i * DS + n
                    dBu = m_dBu[g % 3]
                    dve(lambda e, n=n, dBu=dBu, dtu=dtu: e.tensor_mul(
                            dBu[:], dtu[:], m_Ball[:, n * HC:(n + 1) * HC]),
                        reads=[b(dtu), mballs[n]], writes=[b(dBu)])
                emit_dBu(0)
                for n in range(DS):
                    g = i * DS + n
                    dA = m_dA[g % 8]
                    dBu = m_dBu[g % 3]
                    hh = m_h[g % 3]
                    if n % 8 == 0:
                        for q in range(8):
                            emit_dA(n + q)
                    if SCAN_AS_MUL:
                        dve(lambda e, g=g, dA=dA, dBu=dBu, hh=hh: e.tensor_mul(
                                hh[:], dA[:], dBu[:]),
                            reads=[b(dA), b(dBu), st_bufs[g]], writes=[b(hh)])
                    else:
                        dve(lambda e, g=g, dA=dA, dBu=dBu, hh=hh: e.tensor_tensor_scan(
                                hh[:], dA[:], dBu[:], st_scan[:, g:g + 1], ALU.mult, ALU.add),
                            reads=[b(dA), b(dBu), st_bufs[g]], writes=[b(hh)])
                    if n + 1 < DS:
                        emit_dBu(n + 1)
                    pool(lambda e, g=g, hh=hh: e.tensor_copy(st_scan[:, g:g + 1],
                                                             hh[:, HC - 1:HC]),
                         reads=[b(hh)], writes=[st_bufs[g]])
                    if n == 0:
                        dve(lambda e, n=n, hh=hh: e.tensor_mul(
                                m_yacc[:], hh[:], m_Call[:, n * HC:(n + 1) * HC]),
                            reads=[b(hh), mcalls[n]], writes=[myacc])
                    else:
                        dve(lambda e, n=n, hh=hh: e.tensor_mul(
                                m_ytmp[:], hh[:], m_Call[:, n * HC:(n + 1) * HC]),
                            reads=[b(hh), mcalls[n]], writes=[mytmp])
                        dve(lambda e: e.tensor_add(m_yacc[:], m_yacc[:], m_ytmp[:]),
                            reads=[mytmp], writes=[myacc])
                    if n % 8 == 7 and n + 1 < DS:
                        yield
                dve(lambda e, su_b=su_b, i=i: e.scalar_tensor_tensor(
                        m_yacc[:], su_b[:], c_D[:, i:i + 1], m_yacc[:], ALU.mult, ALU.add),
                    reads=[b(su_b), b(c_D)], writes=[myacc])
                dve(lambda e: e.tensor_scalar(m_ytmp[:], m_yacc[:], 0.0, 1.0, ALU.max, ALU.min),
                    reads=[myacc], writes=[mytmp])
                act(lambda e, sz_b=sz_b: e.activation(m_t2[:, 0:HC], sz_b[:], AF.Silu),
                    reads=[b(sz_b)], writes=[mt2])
                dve(lambda e: e.tensor_mul(m_ytmp[:], m_ytmp[:], m_t2[:, 0:HC]),
                    reads=[mt2], writes=[mytmp])
                yb = ms_y[i % 2]
                dve(lambda e, yb=yb: e.tensor_scalar(yb[:], m_ytmp[:],
                                                     1.0, None, ALU.min),
                    reads=[mytmp], writes=[b(yb)])
                gq_dma(scr_sp["sy"][i][:, off:off + HC], yb[:], reads=[b(yb)],
                       writes=[sp_bufs["sy"][i]], queue="SPY")
                yield

    def wout(c):
        yslots = []
        for g2_ in range(2):
            slot = ring[g2_]
            sy_dma(slot[:, :].rearrange("p (a t) -> p a t", a=8),
                   scr_sp["sy"][8 * g2_:8 * g2_ + 8].rearrange("a p t -> p a t"),
                   reads=[sp_bufs["sy"][8 * g2_ + q2] for q2 in range(8)], writes=[b(slot)],
                   queue="RS%d" % g2_)
            yslots.append(slot)
        def yview(k):
            return yslots[k // 8][:, (k % 8) * TC:(k % 8 + 1) * TC]
        xrs = sbl(x_res, NDM)
        for s in range(4):
            slot = load_wslice("w_outT", s, ring_set=(2, 3))
            for m in range(2):
                et = 2 * s + m
                p = pb[et % 4]
                mm_group(p[:, :], b(p),
                         [(slot[:, k * 256 + m * 128: k * 256 + (m + 1) * 128],
                           yview(k), [b(slot), b(yslots[k // 8])]) for k in range(NDT)])
                dve(lambda e, p=p, et=et: e.tensor_add(x_res[:, et * TC:(et + 1) * TC],
                                                       p[:, :], x0_next[:, et * TC:(et + 1) * TC]),
                    reads=[b(p), b(x0_next)], writes=[xrs[et]])
        if c + 1 < nchunk:
            gq_dma(x0_next[:], x0_in[c + 1].rearrange("p j t -> p (j t)"), writes=[b(x0_next)],
                   queue="X0Q")

    def attnb_fn(c, adv):
        at = f"a{c}"
        rks = sbl(a_rkpad, NDM)
        avts = sbl(a_vtm, 8)
        smts = sbl(a_SmT, 4)
        wkts = sbl(a_wktm, 4)
        ards = sbl(a_reads, NDM)
        xrs = sbl(x_res, NDM)
        for j in range(NDM):
            tagb(rks[j], at, G)
            tagb(ards[j], at, G)
        for q in range(8):
            tagb(avts[q], at, G)
        for q in range(4):
            tagb(smts[q], at, G)
            tagb(wkts[q], at, G)
        at1 = tagb(b(a_t1), at, G)
        for j in range(NDM):
            dve(lambda e, j=j: e.tensor_copy(a_rkpad[:, j * 513: j * 513 + 1],
                                             st_wktail[:, j:j + 1]),
                reads=[b(st_wktail)], writes=[rks[j]])
        layernorm(a_t1, at1, lambda j: a_rkpad[:, j * 513 + 1: j * 513 + 513], rks,
                  lambda j: (a_rkpad[:, j * 513 + 1: j * 513 + 513], rks[j]),
                  lambda j: (a_vtm[:, j * 512:(j + 1) * 512], avts[j]),
                  x_res, xrs, fold=True, fold_bufs=xrs)
        for j in range(NDM):
            dve(lambda e, j=j: e.tensor_copy(st_wktail[:, j:j + 1],
                                             a_rkpad[:, j * 513 + 512: j * 513 + 513]),
                reads=[rks[j]], writes=[b(st_wktail)])
        adv()
        for eh in range(2):
            slot = load_wslice("w_writeT", eh, ring_set=(2, 3))
            for ct in range(4):
                p = pb[ct % 2]
                mm_group(p[:, :], b(p),
                         [(a_rkpad[:, k * 513 + 1 + ct * 128: k * 513 + 1 + (ct + 1) * 128],
                           slot[:, k * 512:(k + 1) * 512], [rks[k], b(slot)]) for k in range(NDM)])
                dve(lambda e, p=p, ct=ct, eh=eh: e.tensor_copy(
                        a_vtm[:, ct * 1024 + eh * 512: ct * 1024 + (eh + 1) * 512], p[:, :]),
                    reads=[b(p)], writes=[avts[2 * ct + eh]])
            adv()
        for kt in range(4):
            p = pb[2 + kt % 2]
            mm_group(p[:, :], b(p),
                     [(a_rkpad[:, k * 513 + kt * 128: k * 513 + (kt + 1) * 128],
                       a_rkpad[:, k * 513 + 1: k * 513 + 513], [rks[k]]) for k in range(NDM)])
            dve(lambda e, p=p, kt=kt: e.tensor_mul(a_SmT[:, kt * TC:(kt + 1) * TC],
                                                   p[:, :], k_M[:, kt * TC:(kt + 1) * TC]),
                reads=[b(p), b(k_M)], writes=[smts[kt]])
        adv()
        for ct in range(4):
            for jd in range(NDM):
                pe(lambda e, ct=ct, jd=jd: e.transpose(
                        pbT[:, 0:128],
                        a_rkpad[:, jd * 513 + ct * 128: jd * 513 + (ct + 1) * 128],
                        c_ident[:]),
                   reads=[rks[jd], b(c_ident)], writes=pbTh)
                dve(lambda e, ct=ct, jd=jd: e.tensor_scalar(
                        a_wktm[:, ct * 1024 + jd * 128: ct * 1024 + (jd + 1) * 128],
                        pbT[:, 0:128],
                        k_gw[:, ct:ct + 1], None, ALU.mult),
                    reads=pbTh, writes=[wkts[ct]])
            adv()
        for j in range(NDM):
            dve(lambda e, j=j: e.tensor_mul(a_rkpad[:, j * 513 + 1: j * 513 + 513],
                                            a_rkpad[:, j * 513 + 1: j * 513 + 513], k_gp[:]),
                reads=[b(k_gp), rks[j]], writes=[rks[j]])
        wtb_slots = []
        for g2_ in range(2):
            slot = ring[g2_]
            sy_dma(slot[:, :].rearrange("p (a t) -> p a t", a=4),
                   scr_WTb[4 * g2_:4 * g2_ + 4].rearrange("a p t -> p a t"),
                   reads=[wtb_bufs[4 * g2_ + q2] for q2 in range(4)], writes=[b(slot)],
                   queue="RS%d" % g2_)
            wtb_slots.append(slot)
        def wtbview(jt, i):
            return wtb_slots[jt // 4][:, (jt % 4) * 1024 + i * 128: (jt % 4) * 1024 + (i + 1) * 128]
        for i in range(NDM):
            p = pb[i % 2]
            pairs = [(wtbview(jt, i),
                      a_rkpad[:, jt * 513 + 1: jt * 513 + 513], [b(wtb_slots[jt // 4]), rks[jt]])
                     for jt in range(8)]
            pairs += [(a_vtm[:, ct * 1024 + i * 128: ct * 1024 + (i + 1) * 128],
                       a_SmT[:, ct * TC:(ct + 1) * TC],
                       [avts[2 * ct + i // 4], smts[ct]]) for ct in range(4)]
            mm_group(p[:, :], b(p), pairs)
            dve(lambda e, p=p, i=i: e.tensor_copy(a_reads[:, i * TC:(i + 1) * TC], p[:, :]),
                reads=[b(p)], writes=[ards[i]])
            if i % 2 == 1:
                adv()
        for eh in range(2):
            slot = load_wslice("w_readT", eh, ring_set=(2, 3))
            for e2 in range(4):
                g2 = 4 * eh + e2
                p = pb[2 + e2 % 2]
                mm_group(p[:, :], b(p),
                         [(slot[:, k * 512 + e2 * 128: k * 512 + (e2 + 1) * 128],
                           a_reads[:, k * TC:(k + 1) * TC], [b(slot), ards[k]])
                          for k in range(NDM)])
                dve(lambda e, p=p, g2=g2: e.scalar_tensor_tensor(
                        x_res[:, g2 * TC:(g2 + 1) * TC], p[:, :], k_alpha[:, 0:1],
                        x_res[:, g2 * TC:(g2 + 1) * TC], ALU.mult, ALU.add),
                    reads=[b(p), b(k_alpha), xrs[g2]], writes=[xrs[g2]])
            adv()
        for jt in range(4):
            awf = aw_f32[jt]
            sc_dma(awf[:], scr_WT[jt], reads=[wt_bufs[jt]], writes=[gb(awf, at)],
                   queue="AW%d" % (jt % 2))
        for jt in range(8):
            awf = aw_f32[jt % 4]
            awb = gb(awf, at)
            abb = gb(aw_bf, at)
            for ih in range(2):
                p = pb_dw if ih == 0 else pb[0]
                mm_group(p[:, :], b(p),
                         [(a_wktm[:, ct * 1024 + jt * 128: ct * 1024 + (jt + 1) * 128],
                           a_vtm[:, ct * 1024 + ih * 512: ct * 1024 + (ih + 1) * 512],
                           [wkts[ct], avts[2 * ct + ih]]) for ct in range(4)])
                dve(lambda e, awf=awf, ih=ih, p=p: e.scalar_tensor_tensor(
                        awf[:, ih * 512:(ih + 1) * 512], awf[:, ih * 512:(ih + 1) * 512],
                        k_gC[:, 0:1], p[:, :], ALU.mult, ALU.add),
                    reads=[awb, b(k_gC), b(p)], writes=[awb])
            dve(lambda e, awf=awf: e.tensor_copy(aw_bf[:], awf[:]),
                reads=[awb], writes=[abb])
            gq_dma(scr_WT[jt], awf[:], reads=[awb], writes=[wt_bufs[jt]], queue="WTS")
            gq_dma(scr_WTb[jt], aw_bf[:], reads=[abb], writes=[wtb_bufs[jt]], queue="WTS")
            if jt + 4 < 8:
                awn = aw_f32[(jt + 4) % 4]
                sc_dma(awn[:], scr_WT[jt + 4], reads=[wt_bufs[jt + 4], gb(awn, at)],
                       writes=[gb(awn, at)], queue="AW%d" % (jt % 2))
            if jt % 2 == 1:
                adv()

    gen_holder = []
    adv_acc = [0.0]
    def adv(w=1.0):
        adv_acc[0] += w
        k = int(adv_acc[0])
        adv_acc[0] -= k
        for _ in range(k):
            if not gen_holder:
                break
            if next(gen_holder[0], "END") == "END":
                gen_holder.pop(0)
    def drain():
        while gen_holder:
            if next(gen_holder[0], "END") == "END":
                gen_holder.pop(0)

    xrs_all = sbl(x_res, NDM)
    for c in range(nchunk):
        if c == 0:
            gen_holder.append(mamba_pre(0))
            gen_holder.append(scan_gen(0))
            drain()
        wout(c)
        if dbg == 1:
            gq_dma(dbg_o["dbg_x1"][c], x_res[:], reads=xrs_all, queue="STQ")
        if c + 1 < nchunk:
            gen_holder.append(mamba_pre(c + 1))
            gen_holder.append(scan_gen(c + 1))
            if DRAIN_STAGE == 0:
                drain()
        ffn(f"f{c}a", adv)
        if DRAIN_STAGE == 1:
            drain()
        if dbg == 1:
            gq_dma(dbg_o["dbg_x2"][c], x_res[:], reads=xrs_all, queue="STQ")
        attnb_fn(c, adv)
        if DRAIN_STAGE == 2:
            drain()
        if dbg == 1:
            gq_dma(dbg_o["dbg_x3"][c], x_res[:], reads=xrs_all, queue="STQ")
        ffn(f"f{c}b", adv)
        drain()
        gq_dma(out_fm[c], x_res[:].rearrange("p (j t) -> p j t", j=NDM),
               reads=xrs_all, queue="STQ")

    if dbg == 2:
        for nm, scr in (("dbg_sdt", scr_sp["sdt"]), ("dbg_su", scr_sp["su"]),
                        ("dbg_sz", scr_sp["sz"]), ("dbg_sy", scr_sp["sy"])):
            gq_dma(dbg_o[nm].rearrange("a p t -> p a t"), scr.rearrange("a p t -> p a t"),
                   reads=[sp_bufs[nm[4:]][q2] for q2 in range(NDT)], queue="STQ")
        gq_dma(dbg_o["dbg_wtb"].rearrange("a p t -> p a t"), scr_WTb.rearrange("a p t -> p a t"),
               reads=wtb_bufs, queue="STQ")
        gq_dma(dbg_o["dbg_wtf"].rearrange("a p t -> p a t"), scr_WT.rearrange("a p t -> p a t"),
               reads=wt_bufs, queue="STQ")
    pl.op("gpsimd", lambda e: e.engine_nop(), extra=[("STQ", pl.count["STQ"])])

    with nc.Block() as block:
        def runner(name):
            own = ENG_SEMS[name]
            skip_self = name == "tensor"
            def run(eng):
                last = {}
                for waits, emit, inc in pl.ops[name]:
                    for s in sorted(waits):
                        v = waits[s]
                        if skip_self and s == own:
                            continue
                        if last.get(s, -1) < v:
                            eng.wait_ge(sems[s], v)
                            last[s] = v
                    inst = emit(eng)
                    if inc is not None:
                        inst.then_inc(sems[inc[0]], inc[1])
            return run
        block.sync(runner("sync"))
        block.tensor(runner("tensor"))
        block.vector(runner("vector"))
        block.scalar(runner("scalar"))
        block.gpsimd(runner("gpsimd"))

    es.close()
    return nc


def _tile_w(WT2, ew):
    K, M = WT2.shape
    kt, ns = K // 128, M // ew
    return np.ascontiguousarray(WT2.reshape(kt, 128, ns, ew).transpose(2, 1, 0, 3))


def prep_inputs(inputs, nchunk=NCHUNK):
    f32 = lambda x: np.ascontiguousarray(np.asarray(x, np.float32))
    W_in = f32(inputs["L0_attn_a_W_in"])
    conv_w = f32(inputs["L0_attn_a_conv_w"]).reshape(DI, 4)
    conv_b = f32(inputs["L0_attn_a_conv_b"])
    W_x = f32(inputs["L0_attn_a_W_x"])
    W_dt = f32(inputs["L0_attn_a_W_dt"])
    b_dt = f32(inputs["L0_attn_a_b_dt"])
    W_out = f32(inputs["L0_attn_a_W_out"])
    A_log = f32(inputs["L0_attn_a_A_log"])
    D_skip = f32(inputs["L0_attn_a_D"])
    W_write = f32(inputs["L0_attn_b_W_write"])
    W_read = f32(inputs["L0_attn_b_W_read"])
    w1_w = f32(inputs["L0_ffn_w1_w"]); w1_s = f32(inputs["L0_ffn_w1_s"])
    w2_w = f32(inputs["L0_ffn_w2_w"]); w2_s = f32(inputs["L0_ffn_w2_s"])
    h1_w = f32(inputs["L0_ffn_h1_w"]); h1_s = f32(inputs["L0_ffn_h1_s"])
    h2_w = f32(inputs["L0_ffn_h2_w"]); h2_s = f32(inputs["L0_ffn_h2_s"])
    w3_w = f32(inputs["L0_ffn_w3_w"]); w3_s = f32(inputs["L0_ffn_w3_s"])

    com = {}
    com["w_inT"] = _tile_w(W_in.T, 512)
    com["w_outT"] = _tile_w(W_out.T, 256)
    com["w_writeT"] = _tile_w(W_write.T, 512)
    com["w_readT"] = _tile_w(W_read.T, 512)
    com["w1T"] = _tile_w(w1_w.T, 512)
    com["w2T"] = _tile_w(w2_w.T, 512)
    com["h1T"] = _tile_w(h1_w.T, 128)
    com["h2T"] = _tile_w(h2_w.T, 128)
    com["w3T"] = _tile_w(w3_w.T, 128)
    def kp(W2, m):
        K = W2.shape[0]
        kt = K // 128
        return np.ascontiguousarray(W2.reshape(kt, 128, m).transpose(1, 0, 2).reshape(128, kt * m))
    com["wx_dtT"] = kp(W_x[0:DTR].T, DTR)
    com["wx_bT"] = kp(W_x[DTR:DTR + 16].T, 16)
    com["wx_cT"] = kp(W_x[DTR + 16:DTR + 32].T, 16)
    com["wdtT"] = np.ascontiguousarray(W_dt.T)
    com["conv_w_l"] = np.ascontiguousarray(conv_w.reshape(NDT, 128, 4).transpose(1, 0, 2).reshape(128, NDT * 4))
    com["conv_b_l"] = np.ascontiguousarray(conv_b.reshape(NDT, 128).T)
    com["D_l"] = np.ascontiguousarray(D_skip.reshape(NDT, 128).T)
    com["bdt_l"] = np.ascontiguousarray(b_dt.reshape(NDT, 128).T)
    com["Alog_l"] = np.ascontiguousarray(A_log.reshape(NDT, 128, DS).transpose(1, 0, 2).reshape(128, NDT * DS))
    com["scales_l"] = np.ascontiguousarray(np.concatenate([
        w1_s.reshape(NH, 128).T, w2_s.reshape(NH, 128).T, h1_s.reshape(NH, 128).T,
        h2_s.reshape(NH, 128).T, w3_s.reshape(NDM, 128).T], axis=1))
    com["sel16"] = np.repeat(np.eye(16, dtype=np.float32)[:, :, None], 128, 2).reshape(16, 2048)
    kk = np.arange(TC)
    karr = (np.arange(4)[:, None, None] * 128 + np.arange(128)[None, :, None])
    cgrid = kk[None, None, :]
    diffs = np.maximum(cgrid - 1 - karr, 0).astype(np.float32)
    causal = (cgrid > karr).astype(np.float32)
    com["diffs_l"] = np.ascontiguousarray(diffs.transpose(1, 0, 2).reshape(128, 4 * TC))
    com["causal_l"] = np.ascontiguousarray(causal.transpose(1, 0, 2).reshape(128, 4 * TC))
    com["pos_row"] = np.arange(TC, dtype=np.float32).reshape(1, TC)
    com["revpos_l"] = np.ascontiguousarray(
        (TC - 1 - (np.arange(4)[:, None] * 128 + np.arange(128)[None, :])).astype(np.float32).T)
    com["ident"] = np.eye(128, dtype=np.float32)
    com["decay_s"] = f32(inputs["L0_attn_b_decay"]).reshape(1, 1)
    com["logalpha_s"] = f32(inputs["L0_attn_b_log_alpha"]).reshape(1, 1)

    hs = f32(inputs["L0_attn_a_hidden_states"])
    x0s = []
    for bb in range(NCORE):
        xfm = hs[bb].T
        x0 = np.ascontiguousarray(
            xfm.reshape(NDM, 128, NCHUNK, TC).transpose(2, 1, 0, 3))[:nchunk]
        x0s.append(x0)
    return com, x0s


def kernel(**inputs):
    com, x0s = prep_inputs(inputs)
    nc = build_program()
    in_maps = [dict(com, x0=x0s[bb]) for bb in range(NCORE)]
    res = run_bass_kernel_spmd(nc, in_maps, core_ids=list(range(NCORE)))
    out = np.empty((NCORE, T, D), np.float32)
    for bb in range(NCORE):
        o = np.asarray(res.results[bb]["out_fm"], np.float32)
        xfm = o.transpose(2, 1, 0, 3).reshape(D, T)
        out[bb] = xfm.T
    return out


if __name__ == "__main__":
    pass
